# version 08
# baseline (speedup 1.0000x reference)
import contextlib
import numpy as np
import ml_dtypes
import concourse.bass as bass
import concourse.mybir as mybir
from concourse.bass_utils import run_bass_kernel_spmd

F32 = mybir.dt.float32
BF16 = mybir.dt.bfloat16
I32 = mybir.dt.int32
AF = mybir.ActivationFunctionType
ALU = mybir.AluOpType
AX = mybir.AxisListType

NC = 8
B = 4
L = 8192
LC = L // NC
TOK = B * LC
D = 1024
DI = 2048
F4 = 2 * DI
NST = 64
DSH = DI // NC
WXW = 2 * NST + DI
EPS = 1e-5
GELU_C = 0.044715
GELU_S = 0.7978845608028654

ENGS = ("sync", "tensor", "vector", "scalar", "gpsimd")


class Prog:

    def __init__(self, nc, block, sems):
        self.nc = nc
        self.block = block
        self.sems = sems
        self.ctr = {k: 0 for k in sems}
        self.pend = {e: [] for e in ENGS}
        self.lastw = {e: {k: 0 for k in sems} for e in ENGS}

    def tok(self, sem):
        return (sem, self.ctr[sem])

    def step(self, eng, fn, waits=(), inc=None, amt=1):
        w = [t for t in waits if t is not None]
        if inc is not None:
            self.ctr[inc] += amt
            t = (inc, self.ctr[inc])
        else:
            t = None
        self.pend[eng].append((fn, w, inc, self.ctr[inc] if inc else 0, amt))
        return t

    def dma(self, queue, out, in_, waits=(), **kw):
        sem = {"sync": "sl", "gpsimd": "sg", "scalar": "sh"}[queue]
        return self.step(queue, lambda e: e.dma_start(out=out, in_=in_, **kw),
                         waits=waits, inc=sem, amt=16)

    def flush(self):
        for e in ENGS:
            steps = self.pend[e]
            self.pend[e] = []
            if not steps:
                continue
            lastw = self.lastw[e]
            sems = self.sems

            def body(eng, steps=steps, lastw=lastw):
                for fn, waits, inc, val, amt in steps:
                    for (s, v) in waits:
                        if v > lastw[s]:
                            eng.wait_ge(sems[s], v)
                            lastw[s] = v
                    try:
                        inst = fn(eng)
                    except Exception:
                        import traceback
                        print("STEP FAILED, defaults:", getattr(fn, "__defaults__", None))
                        raise
                    if inc is not None:
                        inst.then_inc(sems[inc], amt)
            getattr(self.block, e)(body)

    def barrier_wait(self, eng, toks):
        def fn(e):
            class _N:
                def then_inc(self, *_):
                    pass
            return _N()
        return self.step(eng, lambda e: _nopwait(e), waits=toks)


def _nopwait(e):
    class _N:
        def then_inc(self, *_):
            pass
    return _N()



def emit_ln(P, nc_ALU, xt, xc, tmp, st, ceps, extra_waits):
    P.step("vector", lambda v: v.reduce_sum(st[:, 0:1], xt, AX.X),
           waits=extra_waits)
    P.step("vector", lambda v: v.tensor_mul(tmp, xt, xt))
    P.step("vector", lambda v: v.tensor_scalar(st[:, 1:2], st[:, 0:1], 1.0 / D,
                                               None, ALU.mult))
    P.step("vector", lambda v: v.reduce_sum(st[:, 2:3], tmp, AX.X))
    P.step("vector", lambda v: v.tensor_mul(st[:, 3:4], st[:, 1:2], st[:, 1:2]))
    P.step("vector", lambda v: v.tensor_scalar(xc, xt, st[:, 1:2], None,
                                               ALU.subtract))
    t_v = P.step("vector", lambda v: v.tensor_scalar(st[:, 4:5], st[:, 2:3],
                 1.0 / D, st[:, 3:4], ALU.mult, ALU.subtract), inc="sv")
    t_sd = P.step("scalar", lambda sc: sc.activation(out=st[:, 5:6],
                  in_=st[:, 4:5], func=AF.Sqrt, bias=ceps[:, 0:1]),
                  waits=[t_v], inc="se")
    P.step("vector", lambda v: v.reciprocal(st[:, 6:7], st[:, 5:6]),
           waits=[t_sd])
    P.step("vector", lambda v: v.tensor_copy(st[:, 7:8], st[:, 5:6]))
    t_done = P.step("vector", lambda v: v.tensor_scalar(xc, xc, st[:, 6:7],
                    None, ALU.mult), inc="sv")
    return t_done


def build_program(stop_after=None, dbg=False):
    nc = bass.Bass("TRN2", target_bir_lowering=False, debug=False)

    xloc = nc.dram_tensor("xloc", [TOK, D], F32, kind="ExternalInput").ap()
    wi = nc.dram_tensor("wi", [D, F4], F32, kind="ExternalInput").ap()
    bi = nc.dram_tensor("bi", [F4, 1], F32, kind="ExternalInput").ap()
    wx = nc.dram_tensor("wx", [DI, WXW], F32, kind="ExternalInput").ap()
    bx = nc.dram_tensor("bx", [WXW, 1], F32, kind="ExternalInput").ap()
    wdts = nc.dram_tensor("wdts", [NST, DSH], F32, kind="ExternalInput").ap()
    bdts = nc.dram_tensor("bdts", [DSH, 1], F32, kind="ExternalInput").ap()
    alogs = nc.dram_tensor("alogs", [DSH, NST], F32, kind="ExternalInput").ap()
    dps = nc.dram_tensor("dps", [DSH, 1], F32, kind="ExternalInput").ap()
    wo = nc.dram_tensor("wo", [DI, D], F32, kind="ExternalInput").ap()
    bo = nc.dram_tensor("bo", [1, D], F32, kind="ExternalInput").ap()
    gam = nc.dram_tensor("gam", [1, D], F32, kind="ExternalInput").ap()
    bet = nc.dram_tensor("bet", [1, D], F32, kind="ExternalInput").ap()
    l1w = nc.dram_tensor("l1w", [1, D], F32, kind="ExternalInput").ap()
    l1b = nc.dram_tensor("l1b", [1, D], F32, kind="ExternalInput").ap()
    l2w = nc.dram_tensor("l2w", [1, D], F32, kind="ExternalInput").ap()
    l2b = nc.dram_tensor("l2b", [1, D], F32, kind="ExternalInput").ap()
    qkvw = nc.dram_tensor("qkvw", [1536, D], F32, kind="ExternalInput").ap()
    projw = nc.dram_tensor("projw", [D, D], F32, kind="ExternalInput").ap()
    routw = nc.dram_tensor("routw", [8, D], F32, kind="ExternalInput").ap()
    moew1 = nc.dram_tensor("moew1", [8 * D, DI], F32, kind="ExternalInput").ap()
    moew2 = nc.dram_tensor("moew2", [8 * DI, D], F32, kind="ExternalInput").ap()
    cid_in = nc.dram_tensor("cid", [1, 1], I32, kind="ExternalInput").ap()
    out = nc.dram_tensor("out", [TOK, D], F32, kind="ExternalOutput").ap()

    XN = nc.dram_tensor("XN", [TOK, D], F32).ap()
    XST_loc = nc.dram_tensor("XST_loc", [DI, TOK], BF16).ap()
    RST_loc = nc.dram_tensor("RST_loc", [DI, TOK], BF16).ap()
    CT_loc = nc.dram_tensor("CT_loc", [DI, TOK], BF16).ap()
    DBT_loc = nc.dram_tensor("DBT_loc", [128, TOK], F32).ap()
    XST_g = nc.dram_tensor("XST_g", [NC * DI, TOK], BF16).ap()
    RST_g = nc.dram_tensor("RST_g", [NC * DI, TOK], BF16).ap()
    CT_g = nc.dram_tensor("CT_g", [NC * DI, TOK], BF16).ap()
    DBT_g = nc.dram_tensor("DBT_g", [NC * 128, TOK], F32).ap()
    Y3T_loc = nc.dram_tensor("Y3T_loc", [DSH, NC * TOK], BF16).ap()
    Y3T_g = nc.dram_tensor("Y3T_g", [NC * DSH, NC * TOK], BF16).ap()
    XST_my = nc.dram_tensor("XST_my", [DI, TOK], BF16).ap()
    CT_my = nc.dram_tensor("CT_my", [DI, TOK], BF16).ap()
    RST_my = nc.dram_tensor("RST_my", [DI, TOK], BF16).ap()
    Y3T_my = nc.dram_tensor("Y3T_my", [DI, TOK], BF16).ap()
    HA = nc.dram_tensor("HA", [TOK, D], F32).ap()
    MOE = nc.dram_tensor("MOE", [TOK, D], F32).ap()
    dbg_pairs = []
    if dbg:
        for nm, t in [("XN", XN), ("XST_loc", XST_loc), ("RST_loc", RST_loc),
                      ("CT_loc", CT_loc), ("DBT_loc", DBT_loc),
                      ("Y3T_loc", Y3T_loc), ("HA", HA), ("MOE", MOE)]:
            d = nc.dram_tensor(nm + "_dbg", list(t.shape), t.dtype,
                               kind="ExternalOutput").ap()
            dbg_pairs.append((t, d))

    ES0 = contextlib.ExitStack()
    with ES0:
        s_sl = ES0.enter_context(nc.semaphore("sl"))
        s_sg = ES0.enter_context(nc.semaphore("sg"))
        s_sh = ES0.enter_context(nc.semaphore("sh"))
        s_sm = ES0.enter_context(nc.semaphore("sm"))
        s_se = ES0.enter_context(nc.semaphore("se"))
        s_sv = ES0.enter_context(nc.semaphore("sv"))
        s_sp = ES0.enter_context(nc.semaphore("sp"))
        s_sc = ES0.enter_context(nc.semaphore("sc"))
        scid = ES0.enter_context(nc.sbuf_tensor("scid", [1, 1], I32))
        ident = ES0.enter_context(nc.sbuf_tensor("ident", [128, 128], BF16))
        identf = ES0.enter_context(nc.sbuf_tensor("identf", [128, 128], F32))
        ceps = ES0.enter_context(nc.sbuf_tensor("ceps", [128, 1], F32))
        block = ES0.enter_context(nc.Block())
        sems = dict(sl=s_sl, sg=s_sg, sh=s_sh, sm=s_sm, se=s_se, sv=s_sv,
                    sp=s_sp, sc=s_sc)
        P = Prog(nc, block, sems)

        def _stop_here():
            w = [P.tok("sl"), P.tok("sg"), P.tok("sv"), P.tok("se"),
                 P.tok("sm"), P.tok("sp")]
            for (t, d) in dbg_pairs:
                P.dma("gpsimd", d, t, waits=w)
            P.step("sync", lambda s: s.nop(), waits=[P.tok("sl"), P.tok("sg"),
                                                     P.tok("sv"), P.tok("se"),
                                                     P.tok("sm"), P.tok("sp")])
            P.flush()

        def _pbw():
            return [P.tok(s) for s in ("sl", "sg", "sv", "se", "sm", "sp")]


        ESP0 = contextlib.ExitStack()
        with ESP0:
            iot = ESP0.enter_context(nc.sbuf_tensor("iot", [128, 128], F32))
            pid = ESP0.enter_context(nc.sbuf_tensor("pid", [128, 1], F32))
            idf = ESP0.enter_context(nc.sbuf_tensor("idf", [128, 128], F32))
            t_cid = P.dma("sync", scid[:], cid_in)
            P.step("gpsimd", lambda g: g.iota(iot[:], pattern=[[1, 128]], base=0,
                   channel_multiplier=0, allow_small_or_imprecise_dtypes=True))
            t_iota = P.step("gpsimd", lambda g: g.iota(pid[:], pattern=[[0, 1]], base=0,
                            channel_multiplier=1, allow_small_or_imprecise_dtypes=True),
                            inc="sp")
            P.step("vector", lambda v: v.tensor_scalar(idf[:], iot[:], pid[:, 0:1],
                   None, ALU.is_equal), waits=[t_iota])
            t_eps = P.step("vector", lambda v: v.memset(ceps[:], EPS), inc="sv")
            t_id = P.step("vector", lambda v: v.tensor_copy(ident[:], idf[:]), inc="sv")
            t_idf = P.step("vector", lambda v: v.tensor_copy(identf[:], idf[:]), inc="sv")
            P.flush()

        NTQ = 4
        TQ = TOK // NTQ
        a_out_toks = []
        ESA = contextlib.ExitStack()
        with ESA:
            a_xt = ESA.enter_context(nc.sbuf_tensor("a_xt", [128, D], F32))
            a_xc = ESA.enter_context(nc.sbuf_tensor("a_xc", [128, D], F32))
            a_tmp = ESA.enter_context(nc.sbuf_tensor("a_tmp", [128, D], F32))
            a_st = ESA.enter_context(nc.sbuf_tensor("a_st", [128, 8], F32))
            a_xnb = ESA.enter_context(nc.sbuf_tensor("a_xnb", [128, D], BF16))
            a_xnt = ESA.enter_context(nc.sbuf_tensor("a_xnt", [128, 8 * TQ], BF16))
            a_wk = ESA.enter_context(nc.sbuf_tensor("a_wk", [128, 2 * 16 * 128], BF16))
            a_bia = ESA.enter_context(nc.sbuf_tensor("a_bia", [128, 49], F32))
            a_xsq = ESA.enter_context(nc.sbuf_tensor("a_xsq", [128, 16 * TQ], BF16))
            a_ev = ESA.enter_context(nc.sbuf_tensor("a_ev", [128, 2 * 512], F32))
            a_evb = ESA.enter_context(nc.sbuf_tensor("a_evb", [128, 2 * 512], BF16))
            a_ps = ESA.enter_context(nc.psum_tensor("a_ps", [128, 2 * 512], F32))
            a_pt = ESA.enter_context(nc.psum_tensor("a_pt", [128, 2 * 1024], BF16))
            pba = _pbw()
            t_bz = P.step("vector", lambda v: v.memset(a_tmp[:, 0:128], 0.0),
                          waits=pba, inc="sv")
            t_bd1 = P.dma("sync", a_tmp[0:32, 0:128],
                          bi.rearrange("(a p) o -> a (p o)", p=128), waits=[t_bz])
            t_btp1 = P.step("tensor", lambda t: t.transpose(
                a_ps[:, 0:128].bitcast(F32) if False else a_ps[:, 0:128],
                a_tmp[:, 0:128], identf[:]), waits=[t_bd1, t_idf], inc="sm")
            t_bia1 = P.step("vector", lambda v: v.tensor_copy(a_bia[:, 0:32],
                            a_ps[:, 0:32]), waits=[t_btp1], inc="sv")
            t_bz2 = P.step("vector", lambda v: v.memset(a_tmp[:, 0:128], 0.0),
                           waits=[t_btp1], inc="sv")
            t_bd2 = P.dma("sync", a_tmp[0:17, 0:128],
                          bx.rearrange("(a p) o -> a (p o)", p=128), waits=[t_bz2])
            t_btp2 = P.step("tensor", lambda t: t.transpose(
                a_ps[:, 0:128], a_tmp[:, 0:128], identf[:]), waits=[t_bd2], inc="sm")
            t_bia2 = P.step("vector", lambda v: v.tensor_copy(a_bia[:, 32:49],
                            a_ps[:, 0:17]), waits=[t_btp2], inc="sv")

            t_lnlast = None
            t_xnout_prev = None
            t_tp_ev = [None, None]
            t_mm_ev = [None, None]
            t_wk = [None, None]
            t_wk_mm = [None, None]
            t_evb_out = [None, None]
            t_ev_out = [None, None]
            t_xsq_ready = None

            for tq in range(NTQ):
                for tt in range(8):
                    t0 = tq * TQ + tt * 128
                    t_x = P.dma("sync", a_xt[:], xloc[t0:t0 + 128, :],
                                waits=[t_lnlast] + (pba if tq == 0 and tt == 0 else []))
                    emit_ln(P, None, a_xt[:], a_xc[:], a_tmp[:], a_st, ceps,
                            [t_x, t_xnout_prev, t_eps])
                    t_lnlast = P.step("vector", lambda v: v.tensor_copy(a_xnb[:], a_xc[:]),
                                      inc="sv")
                    t_xnout_prev = P.dma("sync", XN[t0:t0 + 128, :], a_xc[:],
                                         waits=[t_lnlast])
                    a_out_toks.append(t_xnout_prev)
                    for dc in range(8):
                        bk = (tt * 8 + dc) % 2
                        t_tp = P.step("tensor", lambda t, dc=dc, bk=bk: t.transpose(
                            a_pt[:, bk * 1024:bk * 1024 + 128],
                            a_xnb[:, dc * 128:(dc + 1) * 128], ident[:]),
                            waits=[t_lnlast, t_id, t_tp_ev[bk]], inc="sm")
                        t_tp_ev[bk] = P.step("vector", lambda v, dc=dc, tt=tt, bk=bk:
                                             v.tensor_copy(
                                                 a_xnt[:, dc * TQ + tt * 128:dc * TQ + tt * 128 + 128],
                                                 a_pt[:, bk * 1024:bk * 1024 + 128]),
                                             waits=[t_tp], inc="sv")
                t_xnt_ready = t_tp_ev[(7 * 8 + 7) % 2]
                t_xnt_ready2 = t_tp_ev[(7 * 8 + 6) % 2]

                t_xst_dma_prev = a_out_toks[-1] if tq > 0 else None
                for fc in range(32):
                    wb = fc % 2
                    t_wl = None
                    for k in range(8):
                        t_wl = P.dma("gpsimd",
                                     a_wk[:, wb * 2048 + k * 128:wb * 2048 + (k + 1) * 128],
                                     wi[k * 128:(k + 1) * 128, fc * 128:(fc + 1) * 128],
                                     waits=[t_wk_mm[wb]] + (pba if tq == 0 and fc < 2 else []))
                    t_wk[wb] = t_wl
                    for F in range(2):
                        pb = (fc * 2 + F) % 2
                        t_mm = None
                        for k in range(8):
                            t_mm = P.step("tensor", lambda t, k=k, wb=wb, pb=pb, F=F:
                                          t.matmul(a_ps[:, pb * 512:(pb + 1) * 512],
                                                   a_wk[:, wb * 2048 + k * 128:wb * 2048 + (k + 1) * 128],
                                                   a_xnt[:, k * TQ + F * 512:k * TQ + F * 512 + 512],
                                                   start=(k == 0), stop=(k == 7)),
                                          waits=[t_wk[wb], t_xnt_ready, t_xnt_ready2,
                                                 t_mm_ev[pb]],
                                          inc="sm" if k == 7 else None)
                        t_wk_mm[wb] = t_mm
                        if fc < 16:
                            t_ev = P.step("scalar", lambda sc, fc=fc, F=F, pb=pb:
                                          sc.activation(
                                              out=a_xsq[:, fc * TQ + F * 512:fc * TQ + F * 512 + 512],
                                              in_=a_ps[:, pb * 512:(pb + 1) * 512],
                                              func=AF.Silu, bias=a_bia[:, fc:fc + 1]),
                                          waits=[t_mm, t_bia1, t_xst_dma_prev], inc="se")
                            t_mm_ev[pb] = t_ev
                            col = tq * TQ + F * 512
                            a_out_toks.append(P.dma("sync",
                                XST_loc[fc * 128:(fc + 1) * 128, col:col + 512],
                                a_xsq[:, fc * TQ + F * 512:fc * TQ + F * 512 + 512],
                                waits=[t_ev]))
                        else:
                            eb = F
                            t_ev = P.step("scalar", lambda sc, fc=fc, F=F, pb=pb, eb=eb:
                                          sc.activation(
                                              out=a_evb[:, eb * 512:(eb + 1) * 512],
                                              in_=a_ps[:, pb * 512:(pb + 1) * 512],
                                              func=AF.Silu, bias=a_bia[:, fc:fc + 1]),
                                          waits=[t_mm, t_bia1, t_evb_out[eb]], inc="se")
                            t_mm_ev[pb] = t_ev
                            col = tq * TQ + F * 512
                            t_o = P.dma("sync",
                                        RST_loc[(fc - 16) * 128:(fc - 15) * 128, col:col + 512],
                                        a_evb[:, eb * 512:(eb + 1) * 512], waits=[t_ev])
                            t_evb_out[eb] = t_o
                            a_out_toks.append(t_o)
                t_xsq_ready = t_mm_ev[0]
                t_xsq_ready2 = t_mm_ev[1]

                for f2 in range(17):
                    wb = f2 % 2
                    t_wl = None
                    for k in range(16):
                        t_wl = P.dma("gpsimd",
                                     a_wk[:, wb * 2048 + k * 128:wb * 2048 + (k + 1) * 128],
                                     wx[k * 128:(k + 1) * 128, f2 * 128:(f2 + 1) * 128],
                                     waits=[t_wk_mm[wb]])
                    t_wk[wb] = t_wl
                    for F in range(2):
                        pb = (f2 * 2 + F) % 2
                        t_mm = None
                        for k in range(16):
                            t_mm = P.step("tensor", lambda t, k=k, wb=wb, pb=pb, F=F:
                                          t.matmul(a_ps[:, pb * 512:(pb + 1) * 512],
                                                   a_wk[:, wb * 2048 + k * 128:wb * 2048 + (k + 1) * 128],
                                                   a_xsq[:, k * TQ + F * 512:k * TQ + F * 512 + 512],
                                                   start=(k == 0), stop=(k == 15)),
                                          waits=[t_wk[wb], t_xsq_ready, t_xsq_ready2,
                                                 t_mm_ev[pb]],
                                          inc="sm" if k == 15 else None)
                        t_wk_mm[wb] = t_mm
                        col = tq * TQ + F * 512
                        if f2 == 0:
                            eb = F
                            t_ev = P.step("vector", lambda v, pb=pb, eb=eb:
                                          v.tensor_scalar(a_ev[:, eb * 512:(eb + 1) * 512],
                                                          a_ps[:, pb * 512:(pb + 1) * 512],
                                                          a_bia[:, 32:33], None, ALU.add),
                                          waits=[t_mm, t_bia2, t_ev_out[eb]], inc="sv")
                            t_mm_ev[pb] = t_ev
                            t_o = P.dma("sync", DBT_loc[:, col:col + 512],
                                        a_ev[:, eb * 512:(eb + 1) * 512], waits=[t_ev])
                            t_ev_out[eb] = t_o
                            a_out_toks.append(t_o)
                        else:
                            eb = F
                            t_ev = P.step("vector", lambda v, f2=f2, pb=pb, eb=eb:
                                          v.tensor_scalar(a_evb[:, eb * 512:(eb + 1) * 512],
                                                          a_ps[:, pb * 512:(pb + 1) * 512],
                                                          a_bia[:, 32 + f2:33 + f2], None, ALU.add),
                                          waits=[t_mm, t_bia2, t_evb_out[eb]], inc="sv")
                            t_mm_ev[pb] = t_ev
                            t_o = P.dma("sync", CT_loc[(f2 - 1) * 128:f2 * 128, col:col + 512],
                                        a_evb[:, eb * 512:(eb + 1) * 512], waits=[t_ev])
                            t_evb_out[eb] = t_o
                            a_out_toks.append(t_o)
            P.flush()

        if stop_after == "A":
            _stop_here()
            return nc
        all_a = [P.tok("sl"), P.tok("sg")]
        t_ag_dbt = P.step("gpsimd", lambda g: g.collective_compute(
            "AllGather", ALU.bypass, replica_groups=[list(range(NC))],
            ins=[DBT_loc], outs=[DBT_g]), waits=all_a, inc="sc")
        t_ag_xst = P.step("gpsimd", lambda g: g.collective_compute(
            "AllGather", ALU.bypass, replica_groups=[list(range(NC))],
            ins=[XST_loc], outs=[XST_g]), inc="sc")
        t_ag_ct = P.step("gpsimd", lambda g: g.collective_compute(
            "AllGather", ALU.bypass, replica_groups=[list(range(NC))],
            ins=[CT_loc], outs=[CT_g]), inc="sc")
        t_ag_rst = P.step("gpsimd", lambda g: g.collective_compute(
            "AllGather", ALU.bypass, replica_groups=[list(range(NC))],
            ins=[RST_loc], outs=[RST_g]), inc="sc")
        P.flush()

        if stop_after == "AG":
            _stop_here()
            return nc
        TB = 4096
        y3_toks = []
        ESB = contextlib.ExitStack()
        with ESB:
            b_A = ESB.enter_context(nc.sbuf_tensor("b_A", [128, 2 * NST], F32))
            b_dp = ESB.enter_context(nc.sbuf_tensor("b_dp", [128, 2], F32))
            b_bdt = ESB.enter_context(nc.sbuf_tensor("b_bdt", [128, 2], F32))
            b_wdt = ESB.enter_context(nc.sbuf_tensor("b_wdt", [64, DSH], F32))
            b_rhs = ESB.enter_context(nc.sbuf_tensor("b_rhs", [64, 2 * 512], F32))
            b_del = ESB.enter_context(nc.sbuf_tensor("b_del", [128, 2 * TB], F32))
            b_w = ESB.enter_context(nc.sbuf_tensor("b_w", [128, 2 * TB], BF16))
            b_xs = ESB.enter_context(nc.sbuf_tensor("b_xs", [128, 2 * TB], BF16))
            b_acc = ESB.enter_context(nc.sbuf_tensor("b_acc", [128, 2 * TB], F32))
            b_dA = ESB.enter_context(nc.sbuf_tensor("b_dA", [128, 2 * TB], F32))
            b_u = ESB.enter_context(nc.sbuf_tensor("b_u", [128, TB], F32))
            b_h = ESB.enter_context(nc.sbuf_tensor("b_h", [128, TB], F32))
            b_rep = ESB.enter_context(nc.sbuf_tensor("b_rep", [128, 2 * TB], BF16))
            b_ub = ESB.enter_context(nc.sbuf_tensor("b_ub", [128, 2 * TB], BF16))
            b_car = ESB.enter_context(nc.sbuf_tensor("b_car", [128, 2 * NST], F32))
            b_tl = ESB.enter_context(nc.sbuf_tensor("b_tl", [128, TB], BF16))
            b_ps = ESB.enter_context(nc.psum_tensor("b_ps", [128, 2 * 512], F32))
            pbb = _pbw()
            tb1 = P.dma("sync", b_A[:, 0:NST], alogs[0:128, :], waits=pbb)
            tb2 = P.dma("sync", b_A[:, NST:2 * NST], alogs[128:256, :], waits=pbb)
            tb3 = P.dma("sync", b_dp[:, 0:1], dps[0:128, :])
            tb4 = P.dma("sync", b_dp[:, 1:2], dps[128:256, :])
            tb5 = P.dma("sync", b_bdt[:, 0:1], bdts[0:128, :])
            tb6 = P.dma("sync", b_bdt[:, 1:2], bdts[128:256, :])
            tb3b = [P.step("gpsimd", lambda g: g.nop(), waits=pbb)]
            tb7 = P.dma("sync", b_wdt[:, :], wdts)
            P.step("vector", lambda v: v.nop(), waits=pbb)
            P.step("scalar", lambda sc: sc.nop(), waits=pbb)
            P.step("tensor", lambda t: t.nop(), waits=pbb)
            t_ae = P.step("scalar", lambda sc: sc.activation(out=b_A[:, :], in_=b_A[:, :],
                          func=AF.Exp), waits=[tb1, tb2], inc="se")
            t_an = P.step("vector", lambda v: v.tensor_scalar(b_A[:, :], b_A[:, :], -1.0,
                          None, ALU.mult), waits=[t_ae], inc="sv")

            cidv = {}

            def mk_cid(engname):
                def fn(e):
                    r = e.alloc_register("cidr_" + engname)
                    e.reg_load(r, scid[0:1, 0:1])
                    cidv[engname] = e.snap(r, min_val=0, max_val=NC - 1)
                    return _nopwait(e)
                return fn
            P.step("sync", mk_cid("sync"), waits=[t_cid])
            for ii in range(NC):
                t_cp_xst = P.step("sync", lambda s, ii=ii: s.dma_start(
                    out=XST_my[ii * DSH:(ii + 1) * DSH, :],
                    in_=XST_g[bass.ds(cidv["sync"] * DSH + ii * DI, DSH), :]),
                    waits=[t_ag_xst], inc="sl", amt=16)
            for ii in range(NC):
                t_cp_ct = P.step("sync", lambda s, ii=ii: s.dma_start(
                    out=CT_my[ii * DSH:(ii + 1) * DSH, :],
                    in_=CT_g[bass.ds(cidv["sync"] * DSH + ii * DI, DSH), :]),
                    waits=[t_ag_ct], inc="sl", amt=16)
            for ii in range(NC):
                t_cp_rst = P.step("sync", lambda s, ii=ii: s.dma_start(
                    out=RST_my[ii * DSH:(ii + 1) * DSH, :],
                    in_=RST_g[bass.ds(cidv["sync"] * DSH + ii * DI, DSH), :]),
                    waits=[t_ag_rst], inc="sl", amt=16)

            t_del_mm_ev = [None, None]
            t_rhs_mm = [None, None]

            t_xs_read = None
            t_w_read = None
            t_tail_prev = None
            t_rep_read = [None, None]
            t_rep_dma = [None, None]
            t_y3_out = None

            for bb in range(B):
                for hh in range(2):
                    iblk = [4 * hh + j for j in range(4)]
                    t_del_done = [None, None]
                    t_w_done = [None, None]
                    t_xs_dma = [None, None]
                    for dg in range(2):
                        for j, ib in enumerate(iblk):
                            t_xs_dma[dg] = P.step("sync", lambda s, ib=ib, j=j, dg=dg, bb=bb:
                                s.dma_start(out=b_xs[:, dg * TB + j * 1024:dg * TB + j * 1024 + 1024],
                                            in_=XST_my[ib * DSH + dg * 128:ib * DSH + dg * 128 + 128,
                                                       bb * 1024:(bb + 1) * 1024]),
                                waits=[t_cp_xst, t_xs_read], inc="sl", amt=16)
                        for Fc in range(8):
                            ib = iblk[Fc // 2]
                            off = (Fc % 2) * 512
                            rb = Fc % 2
                            t_r = P.dma("sync", b_rhs[:, rb * 512:(rb + 1) * 512],
                                        DBT_g[128 * ib:128 * ib + 64,
                                              bb * 1024 + off:bb * 1024 + off + 512],
                                        waits=[t_ag_dbt, t_rhs_mm[rb]])
                            pb = Fc % 2
                            t_mm = P.step("tensor", lambda t, dg=dg, rb=rb, pb=pb:
                                          t.matmul(b_ps[:, pb * 512:(pb + 1) * 512],
                                                   b_wdt[:, dg * 128:(dg + 1) * 128],
                                                   b_rhs[:, rb * 512:(rb + 1) * 512],
                                                   start=True, stop=True),
                                          waits=[t_r, tb7, t_del_mm_ev[pb]], inc="sm")
                            t_rhs_mm[rb] = t_mm
                            t_ev = P.step("vector", lambda v, dg=dg, Fc=Fc, pb=pb:
                                          v.tensor_scalar(b_del[:, dg * TB + Fc * 512:dg * TB + (Fc + 1) * 512],
                                                          b_ps[:, pb * 512:(pb + 1) * 512],
                                                          b_bdt[:, dg:dg + 1], None, ALU.add),
                                          waits=[t_mm, tb5, tb6, t_w_read], inc="sv")
                            t_del_mm_ev[pb] = t_ev
                        dsl = (lambda dg=dg: b_del[:, dg * TB:(dg + 1) * TB])()
                        P.step("scalar", lambda sc, dsl=dsl: sc.activation(out=b_u[:],
                               in_=dsl, func=AF.Abs), waits=[t_del_mm_ev[0], t_del_mm_ev[1], t_tail_prev])
                        P.step("scalar", lambda sc: sc.activation(out=b_u[:], in_=b_u[:, 0:TB],
                               func=AF.Exp, scale=-1.0))
                        t_sp1 = P.step("scalar", lambda sc: sc.activation(out=b_u[:],
                                       in_=b_u[:], func=AF.Ln, bias=1.0), inc="se")
                        P.step("vector", lambda v, dsl=dsl: v.tensor_relu(b_h[:], dsl),
                               waits=[t_sp1])
                        t_spd = P.step("vector", lambda v, dsl=dsl: v.tensor_add(dsl, b_h[:], b_u[:]),
                                       inc="sv")
                        t_del_done[dg] = t_spd
                        t_w = P.step("vector", lambda v, dg=dg, dsl=dsl: v.tensor_mul(
                            b_w[:, dg * TB:(dg + 1) * TB], dsl,
                            b_xs[:, dg * TB:(dg + 1) * TB]),
                            waits=[t_xs_dma[dg]], inc="sv")
                        t_w_done[dg] = t_w
                        P.step("vector", lambda v, dg=dg: v.memset(
                            b_acc[:, dg * TB:(dg + 1) * TB], 0.0))

                    t_scan = None
                    t_scan_dg = [None, None]
                    t_dA_prev = None
                    for n in range(NST):
                        rb = n % 2
                        t_rp = None
                        for j, ib in enumerate(iblk):
                            t_rp = P.step("gpsimd", lambda g, ib=ib, j=j, n=n, bb=bb, rb=rb:
                                g.dma_start(out=b_rep[:, rb * TB + j * 1024:rb * TB + j * 1024 + 1024],
                                            in_=DBT_g[128 * ib + 64 + n:128 * ib + 64 + n + 1,
                                                      bb * 1024:(bb + 1) * 1024].partition_broadcast(128)),
                                waits=[t_ag_dbt, t_rep_read[rb]], inc="sg", amt=16)
                        t_rep_dma[rb] = t_rp
                        for dg in range(2):
                            db = (n * 2 + dg) % 2
                            t_dA = P.step("scalar", lambda sc, dg=dg, n=n, db=db: sc.activation(
                                out=b_dA[:, db * TB:(db + 1) * TB],
                                in_=b_del[:, dg * TB:(dg + 1) * TB],
                                func=AF.Exp, scale=b_A[:, dg * NST + n:dg * NST + n + 1]),
                                waits=[t_del_done[dg], t_an, t_scan_dg[dg]], inc="se")
                            t_u = P.step("vector", lambda v, dg=dg, rb=rb: v.tensor_mul(
                                b_ub[:, dg * TB:(dg + 1) * TB],
                                b_w[:, dg * TB:(dg + 1) * TB],
                                b_rep[:, rb * TB:(rb + 1) * TB]),
                                waits=[t_rp], inc="sv")
                            t_rep_read[rb] = t_u
                            if hh == 0:
                                t_scan = P.step("vector", lambda v, dg=dg, db=db: v.tensor_tensor_scan(
                                    b_h[:], b_dA[:, db * TB:(db + 1) * TB],
                                    b_ub[:, dg * TB:(dg + 1) * TB],
                                    0.0, ALU.mult, ALU.add),
                                    waits=[t_dA], inc="sv")
                            else:
                                t_scan = P.step("vector", lambda v, dg=dg, n=n, db=db:
                                    v.tensor_tensor_scan(
                                        b_h[:], b_dA[:, db * TB:(db + 1) * TB],
                                        b_ub[:, dg * TB:(dg + 1) * TB],
                                        b_car[:, dg * NST + n:dg * NST + n + 1],
                                        ALU.mult, ALU.add),
                                    waits=[t_dA], inc="sv")
                            t_scan_dg[dg] = t_scan
                            P.step("vector", lambda v, dg=dg: v.tensor_add(
                                b_acc[:, dg * TB:(dg + 1) * TB],
                                b_acc[:, dg * TB:(dg + 1) * TB], b_h[:]))
                            if hh == 0:
                                P.step("vector", lambda v, dg=dg, n=n:
                                    v.tensor_copy(b_car[:, dg * NST + n:dg * NST + n + 1],
                                                  b_h[:, TB - 1:TB]))
                            t_dA_prev = t_dA

                    for dg in range(2):
                        t_ct = None
                        for j, ib in enumerate(iblk):
                            t_ct = P.step("sync", lambda s, ib=ib, j=j, dg=dg, bb=bb:
                                s.dma_start(out=b_tl[:, j * 1024:(j + 1) * 1024],
                                            in_=CT_my[ib * DSH + dg * 128:ib * DSH + dg * 128 + 128,
                                                      bb * 1024:(bb + 1) * 1024]),
                                waits=[t_cp_ct, t_tail_prev], inc="sl", amt=16)
                        t_y1 = P.step("vector", lambda v, dg=dg: v.tensor_mul(
                            b_u[:], b_tl[:], b_acc[:, dg * TB:(dg + 1) * TB]),
                            waits=[t_ct], inc="sv")
                        P.step("vector", lambda v, dg=dg: v.scalar_tensor_tensor(
                            b_u[:], b_xs[:, dg * TB:(dg + 1) * TB], b_dp[:, dg:dg + 1],
                            b_u[:], ALU.mult, ALU.add), waits=[tb3, tb4])
                        t_rst = None
                        for j, ib in enumerate(iblk):
                            t_rst = P.step("sync", lambda s, ib=ib, j=j, dg=dg, bb=bb:
                                s.dma_start(out=b_tl[:, j * 1024:(j + 1) * 1024],
                                            in_=RST_my[ib * DSH + dg * 128:ib * DSH + dg * 128 + 128,
                                                       bb * 1024:(bb + 1) * 1024]),
                                waits=[t_cp_rst, t_y1], inc="sl", amt=16)
                        t_y3 = P.step("vector", lambda v: v.tensor_mul(b_tl[:], b_u[:], b_tl[:]),
                                      waits=[t_rst], inc="sv")
                        t_out_prev = None
                        for j, ib in enumerate(iblk):
                            t_out_prev = P.dma("sync",
                                Y3T_loc[dg * 128:(dg + 1) * 128,
                                        ib * TOK + bb * 1024:ib * TOK + (bb + 1) * 1024],
                                b_tl[:, j * 1024:(j + 1) * 1024], waits=[t_y3])
                            y3_toks.append(t_out_prev)
                        t_tail_prev = t_out_prev
                        t_xs_read = t_tail_prev
                        t_w_read = t_scan
            P.flush()

        if stop_after == "B":
            _stop_here()
            return nc
        t_ag_y3 = P.step("gpsimd", lambda g: g.collective_compute(
            "AllGather", ALU.bypass, replica_groups=[list(range(NC))],
            ins=[Y3T_loc], outs=[Y3T_g]), waits=[P.tok("sl")], inc="sc")
        t_cp_y3 = None
        for ii in range(4):
            t_cp_y3 = P.step("sync", lambda s, ii=ii: s.dma_start(
                out=Y3T_my[ii * 512:(ii + 1) * 512, :],
                in_=Y3T_g[ii * 512:(ii + 1) * 512, bass.ds(cidv["sync"] * TOK, TOK)]),
                waits=[t_ag_y3], inc="sl", amt=16)
        P.flush()

        if stop_after == "BG":
            _stop_here()
            return nc
        out_toks = []
        ESC = contextlib.ExitStack()
        with ESC:
            c_l2T = ESC.enter_context(nc.sbuf_tensor("c_l2T", [128, 8 * TOK], BF16))
            c_wexp = ESC.enter_context(nc.sbuf_tensor("c_wexp", [128, 32 * 8], F32))
            c_rep7 = ESC.enter_context(nc.sbuf_tensor("c_rep7", [128, 7 * D], F32))
            c_st = ESC.enter_context(nc.sbuf_tensor("c_st", [128, 8], F32))
            pbc = _pbw()
            P.step("vector", lambda v: v.nop(), waits=pbc)
            P.step("scalar", lambda sc: sc.nop(), waits=pbc)
            P.step("tensor", lambda t: t.nop(), waits=pbc)
            reps = [l1w, l1b, l2w, l2b, gam, bet, bo]
            t_rep = None
            for i, src in enumerate(reps):
                t_rep = P.dma("sync", c_rep7[:, i * D:(i + 1) * D],
                              src.partition_broadcast(128), waits=pbc)
            RW = {"l1w": 0, "l1b": 1, "l2w": 2, "l2b": 3, "gam": 4, "bet": 5, "bo": 6}

            def rsl(name):
                i = RW[name]
                return c_rep7[:, i * D:(i + 1) * D]

            ESC1 = contextlib.ExitStack()
            with ESC1:
                c_qkvT = ESC1.enter_context(nc.sbuf_tensor("c_qkvT", [128, 8 * 1536], BF16))
                c_projT = ESC1.enter_context(nc.sbuf_tensor("c_projT", [128, 8 * D], BF16))
                c_rT = ESC1.enter_context(nc.sbuf_tensor("c_rT", [128, 8 * 8], BF16))
                c_wtmp = ESC1.enter_context(nc.sbuf_tensor("c_wtmp", [128, D], BF16))
                c_rtmp = ESC1.enter_context(nc.sbuf_tensor("c_rtmp", [128, D], BF16))
                c_xn4 = ESC1.enter_context(nc.sbuf_tensor("c_xn4", [128, 4 * D], F32))
                c_xc = ESC1.enter_context(nc.sbuf_tensor("c_xc", [128, D], F32))
                c_t1 = ESC1.enter_context(nc.sbuf_tensor("c_t1", [128, D], F32))
                c_t2 = ESC1.enter_context(nc.sbuf_tensor("c_t2", [128, D], F32))
                c_bf = ESC1.enter_context(nc.sbuf_tensor("c_bf", [128, D], BF16))
                c_lT = ESC1.enter_context(nc.sbuf_tensor("c_lT", [128, 8 * 128], BF16))
                c_q = ESC1.enter_context(nc.sbuf_tensor("c_q", [128, 4 * D], BF16))
                c_k = ESC1.enter_context(nc.sbuf_tensor("c_k", [128, 4 * 256], BF16))
                c_v = ESC1.enter_context(nc.sbuf_tensor("c_v", [128, 4 * 256], F32))
                c_pr = ESC1.enter_context(nc.sbuf_tensor("c_pr", [128, 128], BF16))
                c_sc = ESC1.enter_context(nc.sbuf_tensor("c_sc", [128, 4 * 64], F32))
                c_sm = ESC1.enter_context(nc.sbuf_tensor("c_sm", [128, 48], F32))
                c_me = ESC1.enter_context(nc.sbuf_tensor("c_me", [128, 64], F32))
                c_e = ESC1.enter_context(nc.sbuf_tensor("c_e", [128, 64], F32))
                c_p = ESC1.enter_context(nc.sbuf_tensor("c_p", [128, 4 * 64], F32))
                c_at = ESC1.enter_context(nc.sbuf_tensor("c_at", [128, D], F32))
                c_ps3 = ESC1.enter_context(nc.psum_tensor("c_ps3", [128, 3 * 512], F32))
                c_pt = ESC1.enter_context(nc.psum_tensor("c_pt", [128, 2 * 1024], BF16))
                c_psr = ESC1.enter_context(nc.psum_tensor("c_psr", [128, 8], F32))
                t_tp_ev = [None, None]
                t_wt_dma = None
                for rc in range(12):
                    t_wt_dma = P.dma("gpsimd", c_wtmp[:], qkvw[rc * 128:(rc + 1) * 128, :],
                                     waits=[t_tp_ev[0], t_tp_ev[1]] + (pbc if rc == 0 else []))
                    for dc in range(8):
                        bk = dc % 2
                        t_tp = P.step("tensor", lambda t, dc=dc, bk=bk: t.transpose(
                            c_pt[:, bk * 1024:bk * 1024 + 128],
                            c_wtmp[:, dc * 128:(dc + 1) * 128], ident[:]),
                            waits=[t_wt_dma, t_id, t_tp_ev[bk]], inc="sm")
                        t_tp_ev[bk] = P.step("vector", lambda v, dc=dc, rc=rc, bk=bk:
                            v.tensor_copy(c_qkvT[:, dc * 1536 + rc * 128:dc * 1536 + (rc + 1) * 128],
                                          c_pt[:, bk * 1024:bk * 1024 + 128]),
                            waits=[t_tp], inc="sv")
                for rc in range(8):
                    t_wt_dma = P.dma("gpsimd", c_wtmp[:], projw[rc * 128:(rc + 1) * 128, :],
                                     waits=[t_tp_ev[0], t_tp_ev[1]])
                    for dc in range(8):
                        bk = dc % 2
                        t_tp = P.step("tensor", lambda t, dc=dc, bk=bk: t.transpose(
                            c_pt[:, bk * 1024:bk * 1024 + 128],
                            c_wtmp[:, dc * 128:(dc + 1) * 128], ident[:]),
                            waits=[t_wt_dma, t_tp_ev[bk]], inc="sm")
                        t_tp_ev[bk] = P.step("vector", lambda v, dc=dc, rc=rc, bk=bk:
                            v.tensor_copy(c_projT[:, dc * D + rc * 128:dc * D + rc * 128 + 128],
                                          c_pt[:, bk * 1024:bk * 1024 + 128]),
                            waits=[t_tp], inc="sv")
                t_rt0 = P.step("vector", lambda v: v.memset(c_rtmp[:], 0.0),
                               waits=[t_tp_ev[0], t_tp_ev[1]], inc="sv")
                t_rt_dma = P.dma("gpsimd", c_rtmp[0:8, :], routw, waits=[t_rt0])
                for dc in range(8):
                    bk = dc % 2
                    t_tp = P.step("tensor", lambda t, dc=dc, bk=bk: t.transpose(
                        c_pt[:, bk * 1024:bk * 1024 + 128],
                        c_rtmp[:, dc * 128:(dc + 1) * 128], ident[:]),
                        waits=[t_rt_dma, t_tp_ev[bk]], inc="sm")
                    t_tp_ev[bk] = P.step("vector", lambda v, dc=dc, bk=bk:
                        v.tensor_copy(c_rT[:, dc * 8:(dc + 1) * 8],
                                      c_pt[:, bk * 1024:bk * 1024 + 8]),
                        waits=[t_tp], inc="sv")

                t_sl_c = P.tok("sl")
                t_lT_mm = None
                t_ps3_ev = None
                t_psr_ev = None
                t_ha_out = None
                t_l2T_wr = [None]
                for lc in range(8):
                    t_qkv_ev = [None, None, None, None]
                    for bb in range(B):
                        t0 = bb * LC + lc * 128
                        t_xn = P.dma("sync", c_xn4[:, bb * D:(bb + 1) * D],
                                     XN[t0:t0 + 128, :], waits=[t_ha_out, t_sl_c])
                        emit_ln(P, None, c_xn4[:, bb * D:(bb + 1) * D], c_xc[:],
                                c_t1[:], c_st, ceps, [t_xn, t_rep, t_eps])
                        P.step("vector", lambda v: v.tensor_mul(c_t1[:], c_xc[:], rsl("l1w")))
                        P.step("vector", lambda v: v.tensor_add(c_t1[:], c_t1[:], rsl("l1b")))
                        t_l1 = P.step("vector", lambda v: v.tensor_copy(c_bf[:], c_t1[:]),
                                      inc="sv")
                        for dc in range(8):
                            bk = dc % 2
                            t_tp = P.step("tensor", lambda t, dc=dc, bk=bk: t.transpose(
                                c_pt[:, bk * 1024:bk * 1024 + 128],
                                c_bf[:, dc * 128:(dc + 1) * 128], ident[:]),
                                waits=[t_l1, t_tp_ev[bk], t_lT_mm], inc="sm")
                            t_tp_ev[bk] = P.step("vector", lambda v, dc=dc, bk=bk:
                                v.tensor_copy(c_lT[:, dc * 128:(dc + 1) * 128],
                                              c_pt[:, bk * 1024:bk * 1024 + 128]),
                                waits=[t_tp], inc="sv")
                        t_mm = None
                        for k in range(8):
                            for Fq in range(3):
                                t_mm = P.step("tensor", lambda t, k=k, Fq=Fq:
                                    t.matmul(c_ps3[:, Fq * 512:(Fq + 1) * 512],
                                             c_lT[:, k * 128:(k + 1) * 128],
                                             c_qkvT[:, k * 1536 + Fq * 512:k * 1536 + Fq * 512 + 512],
                                             start=(k == 0), stop=(k == 7)),
                                    waits=[t_tp_ev[0], t_tp_ev[1], t_ps3_ev],
                                    inc="sm" if (k == 7 and Fq == 2) else None)
                        t_lT_mm = t_mm
                        P.step("vector", lambda v, bb=bb: v.tensor_scalar(
                            c_q[:, bb * D:(bb + 1) * D], c_ps3[:, 0:1024], 0.125,
                            None, ALU.mult), waits=[t_mm])
                        P.step("vector", lambda v, bb=bb: v.tensor_copy(
                            c_k[:, bb * 256:(bb + 1) * 256], c_ps3[:, 1024:1280]))
                        t_qkv_ev[bb] = P.step("vector", lambda v, bb=bb: v.tensor_copy(
                            c_v[:, bb * 256:(bb + 1) * 256], c_ps3[:, 1280:1536]), inc="sv")
                        t_ps3_ev = t_qkv_ev[bb]

                    t_scdone = None
                    for s in range(B):
                        for tp in range(s + 1):
                            for h in range(16):
                                kv = h // 4
                                sl2 = h % 2
                                P.step("vector", lambda v, s=s, tp=tp, h=h, kv=kv, sl2=sl2:
                                       v.tensor_mul(c_pr[:, sl2 * 64:(sl2 + 1) * 64],
                                                    c_q[:, s * D + h * 64:s * D + h * 64 + 64],
                                                    c_k[:, tp * 256 + kv * 64:tp * 256 + kv * 64 + 64]),
                                       waits=[t_qkv_ev[s], t_qkv_ev[tp]])
                                if h > 0:
                                    po = (h - 1) % 2
                                    P.step("vector", lambda v, s=s, tp=tp, h=h, po=po:
                                           v.tensor_reduce(c_sc[:, s * 64 + (h - 1) * 4 + tp:s * 64 + (h - 1) * 4 + tp + 1],
                                                           c_pr[:, po * 64:(po + 1) * 64], AX.X, ALU.add))
                            P.step("vector", lambda v: v.tensor_copy(c_me[:, 0:8], c_sm[:, 0:8]))
                            P.step("vector", lambda v, s=s, tp=tp:
                                   v.tensor_reduce(c_sc[:, s * 64 + 15 * 4 + tp:s * 64 + 15 * 4 + tp + 1],
                                                   c_pr[:, 64:128], AX.X, ALU.add))
                    for s in range(B):
                        scs = (lambda s=s: c_sc[:, s * 64:(s + 1) * 64])()
                        gap2 = lambda v: v.tensor_copy(c_me[:, 0:16], c_sm[:, 16:32])
                        P.step("vector", lambda v, s=s, scs=scs: v.tensor_reduce(
                            c_sm[:, 0:16],
                            scs.rearrange("p (h t) -> p h t", t=4)[:, :, 0:s + 1],
                            AX.X, ALU.max))
                        P.step("vector", gap2)
                        for tp in range(s + 1):
                            P.step("vector", lambda v, tp=tp: v.tensor_copy(
                                c_me[:].rearrange("p (h t) -> p h t", t=4)[:, :, tp:tp + 1],
                                c_sm[:, 0:16].rearrange("p (h o) -> p h o", o=1)))
                        P.step("vector", lambda v: v.tensor_copy(c_sm[:, 16:32], c_sm[:, 0:16]))
                        t_sub = P.step("vector", lambda v, scs=scs: v.tensor_sub(
                            c_e[:], scs, c_me[:]), inc="sv")
                        t_exp = P.step("scalar", lambda sc: sc.activation(
                            out=c_e[:], in_=c_e[:], func=AF.Exp), waits=[t_sub], inc="se")
                        P.step("vector", lambda v, s=s: v.tensor_reduce(
                            c_sm[:, 16:32],
                            c_e[:].rearrange("p (h t) -> p h t", t=4)[:, :, 0:s + 1],
                            AX.X, ALU.add), waits=[t_exp])
                        P.step("vector", lambda v: v.tensor_copy(c_me[:, 0:16], c_sm[:, 0:16]))
                        P.step("vector", lambda v: v.reciprocal(c_sm[:, 32:48], c_sm[:, 16:32]))
                        P.step("vector", lambda v: v.tensor_copy(c_me[:, 0:16], c_sm[:, 0:16]))
                        for tp in range(s + 1):
                            P.step("vector", lambda v, s=s, tp=tp: v.tensor_mul(
                                c_p[:, s * 64:(s + 1) * 64].rearrange("p (h t) -> p h t", t=4)[:, :, tp:tp + 1],
                                c_e[:].rearrange("p (h t) -> p h t", t=4)[:, :, tp:tp + 1],
                                c_sm[:, 32:48].rearrange("p (h o) -> p h o", o=1)))
                    t_probs = P.tok("sv")

                    for s in range(B):
                        t0 = s * LC + lc * 128
                        for tp in range(s + 1):
                            for h in range(16):
                                kv = h // 4
                                if tp == 0:
                                    P.step("vector", lambda v, s=s, tp=tp, h=h, kv=kv:
                                        v.tensor_scalar(c_at[:, h * 64:(h + 1) * 64],
                                                        c_v[:, tp * 256 + kv * 64:tp * 256 + kv * 64 + 64],
                                                        c_p[:, s * 64 + h * 4 + tp:s * 64 + h * 4 + tp + 1],
                                                        None, ALU.mult),
                                        waits=[t_probs])
                                else:
                                    P.step("vector", lambda v, s=s, tp=tp, h=h, kv=kv:
                                        v.scalar_tensor_tensor(c_at[:, h * 64:(h + 1) * 64],
                                                               c_v[:, tp * 256 + kv * 64:tp * 256 + kv * 64 + 64],
                                                               c_p[:, s * 64 + h * 4 + tp:s * 64 + h * 4 + tp + 1],
                                                               c_at[:, h * 64:(h + 1) * 64],
                                                               ALU.mult, ALU.add))
                        t_at = P.step("vector", lambda v: v.tensor_copy(c_bf[:], c_at[:]),
                                      inc="sv")
                        for dc in range(8):
                            bk = dc % 2
                            t_tp = P.step("tensor", lambda t, dc=dc, bk=bk: t.transpose(
                                c_pt[:, bk * 1024:bk * 1024 + 128],
                                c_bf[:, dc * 128:(dc + 1) * 128], ident[:]),
                                waits=[t_at, t_tp_ev[bk], t_lT_mm], inc="sm")
                            t_tp_ev[bk] = P.step("vector", lambda v, dc=dc, bk=bk:
                                v.tensor_copy(c_lT[:, dc * 128:(dc + 1) * 128],
                                              c_pt[:, bk * 1024:bk * 1024 + 128]),
                                waits=[t_tp], inc="sv")
                        t_mm = None
                        for k in range(8):
                            for Fp in range(2):
                                t_mm = P.step("tensor", lambda t, k=k, Fp=Fp:
                                    t.matmul(c_ps3[:, Fp * 512:(Fp + 1) * 512],
                                             c_lT[:, k * 128:(k + 1) * 128],
                                             c_projT[:, k * D + Fp * 512:k * D + Fp * 512 + 512],
                                             start=(k == 0), stop=(k == 7)),
                                    waits=[t_tp_ev[0], t_tp_ev[1], t_ps3_ev],
                                    inc="sm" if (k == 7 and Fp == 1) else None)
                        t_lT_mm = t_mm
                        t_ha = P.step("vector", lambda v, s=s: v.tensor_add(
                            c_t2[:], c_ps3[:, 0:1024], c_xn4[:, s * D:(s + 1) * D]),
                            waits=[t_mm], inc="sv")
                        t_ps3_ev = t_ha
                        t_ha_out = P.dma("sync", HA[t0:t0 + 128, :], c_t2[:], waits=[t_ha])
                        out_toks.append(t_ha_out)
                        emit_ln(P, None, c_t2[:], c_xc[:], c_t1[:], c_st, ceps, [t_eps])
                        P.step("vector", lambda v: v.tensor_mul(c_t1[:], c_xc[:], rsl("l2w")))
                        P.step("vector", lambda v: v.tensor_add(c_t1[:], c_t1[:], rsl("l2b")))
                        t_l2 = P.step("vector", lambda v: v.tensor_copy(c_bf[:], c_t1[:]),
                                      inc="sv")
                        tau0 = s * LC + lc * 128
                        for dc in range(8):
                            bk = dc % 2
                            t_tp = P.step("tensor", lambda t, dc=dc, bk=bk: t.transpose(
                                c_pt[:, bk * 1024:bk * 1024 + 128],
                                c_bf[:, dc * 128:(dc + 1) * 128], ident[:]),
                                waits=[t_l2, t_tp_ev[bk], t_lT_mm], inc="sm")
                            t_tp_ev[bk] = P.step("vector", lambda v, dc=dc, bk=bk, tau0=tau0:
                                v.tensor_copy(c_l2T[:, dc * TOK + tau0:dc * TOK + tau0 + 128],
                                              c_pt[:, bk * 1024:bk * 1024 + 128]),
                                waits=[t_tp], inc="sv")
                        t_l2T_wr[0] = t_tp_ev[(7) % 2]
                        t_rmm = None
                        for k in range(8):
                            t_rmm = P.step("tensor", lambda t, k=k, tau0=tau0:
                                t.matmul(c_psr[:],
                                         c_l2T[:, k * TOK + tau0:k * TOK + tau0 + 128],
                                         c_rT[:, k * 8:(k + 1) * 8],
                                         start=(k == 0), stop=(k == 7)),
                                waits=[t_tp_ev[0], t_tp_ev[1], t_psr_ev],
                                inc="sm" if k == 7 else None)
                        ti = s * 8 + lc
                        gap = lambda v: v.tensor_copy(c_sm[:, 40:48], c_st[:, 0:8])
                        P.step("vector", lambda v: v.reduce_max(c_st[:, 4:5], c_psr[:], AX.X),
                               waits=[t_rmm])
                        P.step("vector", gap)
                        t_rl = P.step("vector", lambda v: v.tensor_scalar(
                            c_sm[:, 0:8], c_psr[:], c_st[:, 4:5], None, ALU.subtract),
                            inc="sv")
                        t_psr_ev = t_rl
                        t_re = P.step("scalar", lambda sc: sc.activation(
                            out=c_sm[:, 0:8], in_=c_sm[:, 0:8], func=AF.Exp),
                            waits=[t_rl], inc="se")
                        P.step("vector", lambda v: v.reduce_sum(c_st[:, 5:6], c_sm[:, 0:8], AX.X),
                               waits=[t_re])
                        P.step("vector", gap)
                        P.step("vector", lambda v: v.reciprocal(c_st[:, 6:7], c_st[:, 5:6]))
                        P.step("vector", gap)
                        P.step("vector", lambda v: v.tensor_scalar(
                            c_sm[:, 0:8], c_sm[:, 0:8], c_st[:, 6:7], None, ALU.mult))
                        P.step("vector", gap)
                        P.step("vector", lambda v: v.reduce_max(c_st[:, 4:5], c_sm[:, 0:8], AX.X))
                        P.step("vector", gap)
                        P.step("vector", lambda v: v.tensor_scalar(
                            c_sm[:, 8:16], c_sm[:, 0:8], c_st[:, 4:5], None, ALU.is_equal))
                        P.step("vector", gap)
                        P.step("vector", lambda v: v.tensor_scalar(
                            c_sm[:, 8:16], c_sm[:, 8:16], -1e30, None, ALU.mult))
                        P.step("vector", gap)
                        P.step("vector", lambda v: v.tensor_add(
                            c_sm[:, 8:16], c_sm[:, 8:16], c_sm[:, 0:8]))
                        P.step("vector", gap)
                        P.step("vector", lambda v: v.reduce_max(c_st[:, 5:6], c_sm[:, 8:16], AX.X))
                        P.step("vector", gap)
                        P.step("vector", lambda v: v.tensor_scalar(
                            c_sm[:, 8:16], c_sm[:, 0:8], c_st[:, 5:6], None, ALU.is_ge))
                        P.step("vector", gap)
                        P.step("vector", lambda v, ti=ti: v.tensor_mul(
                            c_wexp[:, ti * 8:(ti + 1) * 8], c_sm[:, 0:8], c_sm[:, 8:16]))
            P.flush()

            if stop_after == "C1":
                _stop_here()
                return nc
            t_l2T_all = P.tok("sv")
            ESC2 = contextlib.ExitStack()
            with ESC2:
                c_w1 = ESC2.enter_context(nc.sbuf_tensor("c_w1", [128, 8 * DI], BF16))
                c_w2 = ESC2.enter_context(nc.sbuf_tensor("c_w2", [128, 16 * D], BF16))
                c_hm = ESC2.enter_context(nc.sbuf_tensor("c_hm", [128, 16 * 512], BF16))
                c_g1 = ESC2.enter_context(nc.sbuf_tensor("c_g1", [128, 512], F32))
                c_g2 = ESC2.enter_context(nc.sbuf_tensor("c_g2", [128, 512], F32))
                c_mo = ESC2.enter_context(nc.sbuf_tensor("c_mo", [128, 2 * 512], F32))
                c_pm = ESC2.enter_context(nc.psum_tensor("c_pm", [128, 2 * 512], F32))
                pbc2 = _pbw()
                P.step("vector", lambda v: v.nop(), waits=pbc2)
                P.step("scalar", lambda sc: sc.nop(), waits=pbc2)
                P.step("tensor", lambda t: t.nop(), waits=pbc2)
                t_w1_mm = None
                t_w2_mm = None
                t_pm_ev = [None, None]
                t_ar_prev = None
                t_mo_out = [None, None]
                t_mo_grid = [[None, None] for _ in range(32)]
                for e in range(8):
                    t_w1 = None
                    for k in range(8):
                        t_w1 = P.dma("gpsimd", c_w1[:, k * DI:(k + 1) * DI],
                                     moew1[e * D + k * 128:e * D + (k + 1) * 128, :],
                                     waits=[t_w1_mm] + (pbc2 if e == 0 else []))
                    t_w2 = None
                    for k in range(16):
                        t_w2 = P.dma("gpsimd", c_w2[:, k * D:(k + 1) * D],
                                     moew2[e * DI + k * 128:e * DI + (k + 1) * 128, :],
                                     waits=[t_w2_mm] + (pbc2 if e == 0 else []))
                    for tb in range(8):
                        for fc in range(16):
                            pb = fc % 2
                            t_mm = None
                            for k in range(8):
                                t_mm = P.step("tensor", lambda t, k=k, fc=fc, tb=tb, pb=pb:
                                    t.matmul(c_pm[:, pb * 512:(pb + 1) * 512],
                                             c_w1[:, k * DI + fc * 128:k * DI + fc * 128 + 128],
                                             c_l2T[:, k * TOK + tb * 512:k * TOK + tb * 512 + 512],
                                             start=(k == 0), stop=(k == 7)),
                                    waits=[t_w1, t_l2T_all, t_pm_ev[pb]],
                                    inc="sm" if k == 7 else None)
                            t_w1_mm = t_mm
                            t_sq = P.step("scalar", lambda sc, pb=pb: sc.activation(
                                out=c_g1[:], in_=c_pm[:, pb * 512:(pb + 1) * 512],
                                func=AF.Square), waits=[t_mm, t_ar_prev], inc="se")
                            P.step("vector", lambda v: v.tensor_scalar(
                                c_g2[:], c_g1[:], GELU_C, 1.0, ALU.mult, ALU.add),
                                waits=[t_sq])
                            t_ar = P.step("vector", lambda v, pb=pb: v.tensor_mul(
                                c_g2[:], c_g2[:], c_pm[:, pb * 512:(pb + 1) * 512]),
                                inc="sv")
                            t_ar_prev = t_ar
                            t_th = P.step("scalar", lambda sc: sc.activation(
                                out=c_g2[:], in_=c_g2[:], func=AF.Tanh, scale=GELU_S),
                                waits=[t_ar], inc="se")
                            P.step("vector", lambda v: v.tensor_scalar(
                                c_g2[:], c_g2[:], 0.5, 0.5, ALU.mult, ALU.add),
                                waits=[t_th])
                            t_hm = P.step("vector", lambda v, fc=fc, pb=pb: v.tensor_mul(
                                c_hm[:, fc * 512:(fc + 1) * 512], c_g2[:],
                                c_pm[:, pb * 512:(pb + 1) * 512]), inc="sv")
                            t_pm_ev[pb] = t_hm
                        t_hm_all = t_pm_ev[1]
                        t_hm_all0 = t_pm_ev[0]
                        for tt in range(4):
                            ti = tb * 4 + tt
                            t_mm = None
                            for k2 in range(16):
                                for Fh in range(2):
                                    t_mm = P.step("tensor", lambda t, k2=k2, tt=tt, Fh=Fh:
                                        t.matmul(c_pm[:, Fh * 512:(Fh + 1) * 512],
                                                 c_hm[:, k2 * 512 + tt * 128:k2 * 512 + tt * 128 + 128],
                                                 c_w2[:, k2 * D + Fh * 512:k2 * D + Fh * 512 + 512],
                                                 start=(k2 == 0), stop=(k2 == 15)),
                                        waits=[t_w2, t_hm_all, t_hm_all0,
                                               t_pm_ev[0], t_pm_ev[1]],
                                        inc="sm" if (k2 == 15 and Fh == 1) else None)
                            t_w2_mm = t_mm
                            t_mm_both = t_mm
                            for Fh in range(2):
                                t_ev = P.step("vector", lambda v, ti=ti, e=e, Fh=Fh:
                                    v.tensor_scalar(c_mo[:, Fh * 512:(Fh + 1) * 512],
                                                    c_pm[:, Fh * 512:(Fh + 1) * 512],
                                                    c_wexp[:, ti * 8 + e:ti * 8 + e + 1],
                                                    None, ALU.mult),
                                    waits=[t_mm_both, t_mo_out[Fh]], inc="sv")
                                t_pm_ev[Fh] = t_ev
                                tau0 = ti * 128
                                if e == 0:
                                    t_mo = P.dma("gpsimd",
                                                 MOE[tau0:tau0 + 128, Fh * 512:(Fh + 1) * 512],
                                                 c_mo[:, Fh * 512:(Fh + 1) * 512],
                                                 waits=[t_ev])
                                else:
                                    t_mo = P.dma("gpsimd",
                                                 MOE[tau0:tau0 + 128, Fh * 512:(Fh + 1) * 512],
                                                 c_mo[:, Fh * 512:(Fh + 1) * 512],
                                                 waits=[t_ev, t_mo_grid[ti][Fh]],
                                                 accum_op=ALU.add)
                                t_mo_grid[ti][Fh] = t_mo
                                t_mo_out[Fh] = t_mo
                t_moe_done = P.tok("sg")
                P.flush()

            if stop_after == "C2":
                _stop_here()
                return nc
            ESC3 = contextlib.ExitStack()
            with ESC3:
                c_wo = ESC3.enter_context(nc.sbuf_tensor("c_wo", [128, 16 * D], BF16))
                c_y3 = ESC3.enter_context(nc.sbuf_tensor("c_y3", [128, 16 * 128], BF16))
                c_z = ESC3.enter_context(nc.sbuf_tensor("c_z", [128, D], F32))
                c_xc3 = ESC3.enter_context(nc.sbuf_tensor("c_xc3", [128, D], F32))
                c_tt = ESC3.enter_context(nc.sbuf_tensor("c_tt", [128, D], F32))
                c_s = ESC3.enter_context(nc.sbuf_tensor("c_s", [128, D], F32))
                c_g = ESC3.enter_context(nc.sbuf_tensor("c_g", [128, D], F32))
                c_x3 = ESC3.enter_context(nc.sbuf_tensor("c_x3", [128, D], F32))
                c_st3 = ESC3.enter_context(nc.sbuf_tensor("c_st3", [128, 8], F32))
                c_pw = ESC3.enter_context(nc.psum_tensor("c_pw", [128, 2 * 512], F32))
                t_ha_all = P.tok("sl")
                pbc3 = _pbw()
                P.step("vector", lambda v: v.nop(), waits=pbc3)
                P.step("scalar", lambda sc: sc.nop(), waits=pbc3)
                P.step("tensor", lambda t: t.nop(), waits=pbc3)
                t_wo = None
                for k in range(16):
                    t_wo = P.dma("gpsimd", c_wo[:, k * D:(k + 1) * D],
                                 wo[k * 128:(k + 1) * 128, :], waits=pbc3)
                t_y3_mm = None
                t_pw_ev = None
                t_fin = None
                for ti in range(32):
                    tau0 = ti * 128
                    t_y3d = None
                    for k2 in range(16):
                        t_y3d = P.step("sync", lambda s, k2=k2, ti=ti:
                            s.dma_start(out=c_y3[:, k2 * 128:(k2 + 1) * 128],
                                        in_=Y3T_my[k2 * 128:(k2 + 1) * 128,
                                                   ti * 128:(ti + 1) * 128]),
                            waits=[t_cp_y3, t_y3_mm] + (pbc3 if ti == 0 else []),
                            inc="sl", amt=16)
                    t_mm = None
                    for k2 in range(16):
                        for Fh in range(2):
                            t_mm = P.step("tensor", lambda t, k2=k2, Fh=Fh:
                                t.matmul(c_pw[:, Fh * 512:(Fh + 1) * 512],
                                         c_y3[:, k2 * 128:(k2 + 1) * 128],
                                         c_wo[:, k2 * D + Fh * 512:k2 * D + Fh * 512 + 512],
                                         start=(k2 == 0), stop=(k2 == 15)),
                                waits=[t_y3d, t_wo, t_pw_ev],
                                inc="sm" if (k2 == 15 and Fh == 1) else None)
                    t_y3_mm = t_mm
                    t_xn3 = P.dma("sync", c_x3[:], XN[tau0:tau0 + 128, :],
                                  waits=[t_fin, t_ha_all])
                    P.step("vector", lambda v: v.tensor_add(c_z[:], c_pw[:, 0:1024],
                           rsl("bo")), waits=[t_mm, t_rep, t_xn3])
                    t_z = P.step("vector", lambda v: v.tensor_add(c_z[:], c_z[:], c_x3[:]),
                                 inc="sv")
                    t_pw_ev = t_z
                    t_var = emit_ln(P, None, c_z[:], c_xc3[:], c_tt[:], c_st3,
                                    ceps, [t_eps])
                    P.step("vector", lambda v: v.tensor_mul(c_s[:], c_xc3[:], rsl("gam")))
                    P.step("vector", lambda v: v.tensor_add(c_s[:], c_s[:], rsl("bet")))
                    t_ha3 = P.dma("sync", c_tt[:], HA[tau0:tau0 + 128, :],
                                  waits=[t_var, t_ha_all])
                    t_mo3 = P.dma("sync", c_g[:], MOE[tau0:tau0 + 128, :],
                                  waits=[t_moe_done])
                    t_ff = P.step("vector", lambda v: v.tensor_add(c_g[:], c_g[:], c_tt[:]),
                                  waits=[t_ha3, t_mo3], inc="sv")
                    t_si = P.step("scalar", lambda sc: sc.activation(
                        out=c_g[:], in_=c_g[:], func=AF.Silu), waits=[t_ff], inc="se")
                    t_x = P.dma("sync", c_x3[:], xloc[tau0:tau0 + 128, :], waits=[t_z])
                    P.step("vector", lambda v: v.tensor_mul(c_g[:], c_g[:], c_s[:]),
                           waits=[t_si, t_x])
                    t_o = P.step("vector", lambda v: v.tensor_add(c_xc3[:], c_g[:], c_x3[:]),
                                 inc="sv")
                    t_fin = P.dma("sync", out[tau0:tau0 + 128, :], c_xc3[:], waits=[t_o])
                P.step("sync", lambda s: _nopwait(s), waits=[P.tok("sl"), P.tok("sg")])
                P.flush()

    return nc


_NC_CACHE = {}


def kernel(L0_attn_x, L0_attn_Wi, L0_attn_bi, L0_attn_Wx, L0_attn_bx,
           L0_attn_Wdt, L0_attn_bdt, L0_attn_A_log, L0_attn_Dp, L0_attn_Wo,
           L0_attn_bo, L0_attn_gamma, L0_attn_beta, L0_ffn_ln1_weight,
           L0_ffn_ln1_bias, L0_ffn_ln2_weight, L0_ffn_ln2_bias,
           L0_ffn_qkv_weight, L0_ffn_proj_weight, L0_ffn_router_weight,
           L0_ffn_moe_w1, L0_ffn_moe_w2):
    x = np.ascontiguousarray(np.asarray(L0_attn_x, dtype=np.float32))
    f32 = lambda a: np.ascontiguousarray(np.asarray(a, dtype=np.float32))
    wi = f32(L0_attn_Wi)
    bi = f32(L0_attn_bi).reshape(F4, 1)
    wx = f32(L0_attn_Wx)
    bx = f32(L0_attn_bx).reshape(WXW, 1)
    wdt = f32(L0_attn_Wdt)
    bdt = f32(L0_attn_bdt).reshape(DI, 1)
    alog = f32(L0_attn_A_log)
    dp = f32(L0_attn_Dp).reshape(DI, 1)
    wo = f32(L0_attn_Wo)
    bo = f32(L0_attn_bo).reshape(1, D)
    gam = f32(L0_attn_gamma).reshape(1, D)
    bet = f32(L0_attn_beta).reshape(1, D)
    l1w = f32(L0_ffn_ln1_weight).reshape(1, D)
    l1b = f32(L0_ffn_ln1_bias).reshape(1, D)
    l2w = f32(L0_ffn_ln2_weight).reshape(1, D)
    l2b = f32(L0_ffn_ln2_bias).reshape(1, D)
    qkvw = f32(L0_ffn_qkv_weight)
    projw = f32(L0_ffn_proj_weight)
    routw = f32(L0_ffn_router_weight)
    moew1 = f32(L0_ffn_moe_w1).reshape(8 * D, DI)
    moew2 = f32(L0_ffn_moe_w2).reshape(8 * DI, D)

    if "nc" not in _NC_CACHE:
        _NC_CACHE["nc"] = build_program()
    nc = _NC_CACHE["nc"]

    in_maps = []
    for c in range(NC):
        sl = slice(LC * c, LC * (c + 1))
        dsl = slice(DSH * c, DSH * (c + 1))
        in_maps.append(dict(
            xloc=np.ascontiguousarray(x[:, sl, :].reshape(TOK, D)),
            wi=wi, bi=bi, wx=wx, bx=bx,
            wdts=np.ascontiguousarray(wdt[:, dsl]),
            bdts=np.ascontiguousarray(bdt[dsl]),
            alogs=np.ascontiguousarray(alog[dsl]),
            dps=np.ascontiguousarray(dp[dsl]),
            wo=wo, bo=bo, gam=gam, bet=bet,
            l1w=l1w, l1b=l1b, l2w=l2w, l2b=l2b,
            qkvw=qkvw, projw=projw, routw=routw,
            moew1=moew1, moew2=moew2,
            cid=np.array([[c]], dtype=np.int32),
        ))
    res = run_bass_kernel_spmd(nc, in_maps, core_ids=list(range(NC)))
    outp = np.empty((B, L, D), dtype=np.float32)
    for c in range(NC):
        outp[:, LC * c:LC * (c + 1), :] = res.results[c]["out"].reshape(B, LC, D)
    return outp


# version 09
# speedup vs baseline: 1.0147x; 1.0147x over previous
import contextlib
import numpy as np
import ml_dtypes
import concourse.bass as bass
import concourse.mybir as mybir
from concourse.bass_utils import run_bass_kernel_spmd

F32 = mybir.dt.float32
BF16 = mybir.dt.bfloat16
I32 = mybir.dt.int32
AF = mybir.ActivationFunctionType
ALU = mybir.AluOpType
AX = mybir.AxisListType

NC = 8
B = 4
L = 8192
LC = L // NC
TOK = B * LC
D = 1024
DI = 2048
F4 = 2 * DI
NST = 64
DSH = DI // NC
WXW = 2 * NST + DI
EPS = 1e-5
GELU_C = 0.044715
GELU_S = 0.7978845608028654

ENGS = ("sync", "tensor", "vector", "scalar", "gpsimd")


class Prog:

    def __init__(self, nc, block, sems):
        self.nc = nc
        self.block = block
        self.sems = sems
        self.ctr = {k: 0 for k in sems}
        self.pend = {e: [] for e in ENGS}
        self.lastw = {e: {k: 0 for k in sems} for e in ENGS}

    def tok(self, sem):
        return (sem, self.ctr[sem])

    def step(self, eng, fn, waits=(), inc=None, amt=1):
        w = [t for t in waits if t is not None]
        if inc is not None:
            self.ctr[inc] += amt
            t = (inc, self.ctr[inc])
        else:
            t = None
        self.pend[eng].append((fn, w, inc, self.ctr[inc] if inc else 0, amt))
        return t

    def dma(self, queue, out, in_, waits=(), **kw):
        sem = {"sync": "sl", "gpsimd": "sg", "scalar": "sh"}[queue]
        return self.step(queue, lambda e: e.dma_start(out=out, in_=in_, **kw),
                         waits=waits, inc=sem, amt=16)

    def flush(self):
        for e in ENGS:
            steps = self.pend[e]
            self.pend[e] = []
            if not steps:
                continue
            lastw = self.lastw[e]
            sems = self.sems

            def body(eng, steps=steps, lastw=lastw):
                for fn, waits, inc, val, amt in steps:
                    for (s, v) in waits:
                        if v > lastw[s]:
                            eng.wait_ge(sems[s], v)
                            lastw[s] = v
                    try:
                        inst = fn(eng)
                    except Exception:
                        import traceback
                        print("STEP FAILED, defaults:", getattr(fn, "__defaults__", None))
                        raise
                    if inc is not None:
                        inst.then_inc(sems[inc], amt)
            getattr(self.block, e)(body)

    def barrier_wait(self, eng, toks):
        def fn(e):
            class _N:
                def then_inc(self, *_):
                    pass
            return _N()
        return self.step(eng, lambda e: _nopwait(e), waits=toks)


def _nopwait(e):
    class _N:
        def then_inc(self, *_):
            pass
    return _N()



def emit_ln(P, nc_ALU, xt, xc, tmp, st, ceps, extra_waits):
    P.step("vector", lambda v: v.reduce_sum(st[:, 0:1], xt, AX.X),
           waits=extra_waits)
    P.step("vector", lambda v: v.tensor_mul(tmp, xt, xt))
    P.step("vector", lambda v: v.tensor_scalar(st[:, 1:2], st[:, 0:1], 1.0 / D,
                                               None, ALU.mult))
    P.step("vector", lambda v: v.reduce_sum(st[:, 2:3], tmp, AX.X))
    P.step("vector", lambda v: v.tensor_mul(st[:, 3:4], st[:, 1:2], st[:, 1:2]))
    P.step("vector", lambda v: v.tensor_scalar(xc, xt, st[:, 1:2], None,
                                               ALU.subtract))
    t_v = P.step("vector", lambda v: v.tensor_scalar(st[:, 4:5], st[:, 2:3],
                 1.0 / D, st[:, 3:4], ALU.mult, ALU.subtract), inc="sv")
    t_sd = P.step("scalar", lambda sc: sc.activation(out=st[:, 5:6],
                  in_=st[:, 4:5], func=AF.Sqrt, bias=ceps[:, 0:1]),
                  waits=[t_v], inc="se")
    P.step("vector", lambda v: v.reciprocal(st[:, 6:7], st[:, 5:6]),
           waits=[t_sd])
    P.step("vector", lambda v: v.tensor_copy(st[:, 7:8], st[:, 5:6]))
    t_done = P.step("vector", lambda v: v.tensor_scalar(xc, xc, st[:, 6:7],
                    None, ALU.mult), inc="sv")
    return t_done


def build_program(stop_after=None, dbg=False):
    nc = bass.Bass("TRN2", target_bir_lowering=False, debug=False)

    xloc = nc.dram_tensor("xloc", [TOK, D], F32, kind="ExternalInput").ap()
    wi = nc.dram_tensor("wi", [D, F4], F32, kind="ExternalInput").ap()
    bi = nc.dram_tensor("bi", [F4, 1], F32, kind="ExternalInput").ap()
    wx = nc.dram_tensor("wx", [DI, WXW], F32, kind="ExternalInput").ap()
    bx = nc.dram_tensor("bx", [WXW, 1], F32, kind="ExternalInput").ap()
    wdts = nc.dram_tensor("wdts", [NST, DSH], F32, kind="ExternalInput").ap()
    bdts = nc.dram_tensor("bdts", [DSH, 1], F32, kind="ExternalInput").ap()
    alogs = nc.dram_tensor("alogs", [DSH, NST], F32, kind="ExternalInput").ap()
    dps = nc.dram_tensor("dps", [DSH, 1], F32, kind="ExternalInput").ap()
    wo = nc.dram_tensor("wo", [DI, D], F32, kind="ExternalInput").ap()
    bo = nc.dram_tensor("bo", [1, D], F32, kind="ExternalInput").ap()
    gam = nc.dram_tensor("gam", [1, D], F32, kind="ExternalInput").ap()
    bet = nc.dram_tensor("bet", [1, D], F32, kind="ExternalInput").ap()
    l1w = nc.dram_tensor("l1w", [1, D], F32, kind="ExternalInput").ap()
    l1b = nc.dram_tensor("l1b", [1, D], F32, kind="ExternalInput").ap()
    l2w = nc.dram_tensor("l2w", [1, D], F32, kind="ExternalInput").ap()
    l2b = nc.dram_tensor("l2b", [1, D], F32, kind="ExternalInput").ap()
    qkvw = nc.dram_tensor("qkvw", [1536, D], F32, kind="ExternalInput").ap()
    projw = nc.dram_tensor("projw", [D, D], F32, kind="ExternalInput").ap()
    routw = nc.dram_tensor("routw", [8, D], F32, kind="ExternalInput").ap()
    moew1 = nc.dram_tensor("moew1", [8 * D, DI], F32, kind="ExternalInput").ap()
    moew2 = nc.dram_tensor("moew2", [8 * DI, D], F32, kind="ExternalInput").ap()
    cid_in = nc.dram_tensor("cid", [1, 1], I32, kind="ExternalInput").ap()
    out = nc.dram_tensor("out", [TOK, D], F32, kind="ExternalOutput").ap()

    XN = nc.dram_tensor("XN", [TOK, D], F32).ap()
    XST_loc = nc.dram_tensor("XST_loc", [DI, TOK], BF16).ap()
    RST_loc = nc.dram_tensor("RST_loc", [DI, TOK], BF16).ap()
    CT_loc = nc.dram_tensor("CT_loc", [DI, TOK], BF16).ap()
    DBT_loc = nc.dram_tensor("DBT_loc", [128, TOK], F32).ap()
    XST_g = nc.dram_tensor("XST_g", [NC * DI, TOK], BF16).ap()
    RST_g = nc.dram_tensor("RST_g", [NC * DI, TOK], BF16).ap()
    CT_g = nc.dram_tensor("CT_g", [NC * DI, TOK], BF16).ap()
    DBT_g = nc.dram_tensor("DBT_g", [NC * 128, TOK], F32).ap()
    Y3T_loc = nc.dram_tensor("Y3T_loc", [DSH, NC * TOK], BF16).ap()
    Y3T_g = nc.dram_tensor("Y3T_g", [NC * DSH, NC * TOK], BF16).ap()
    XST_my = nc.dram_tensor("XST_my", [DI, TOK], BF16).ap()
    CT_my = nc.dram_tensor("CT_my", [DI, TOK], BF16).ap()
    RST_my = nc.dram_tensor("RST_my", [DI, TOK], BF16).ap()
    Y3T_my = nc.dram_tensor("Y3T_my", [DI, TOK], BF16).ap()
    HA = nc.dram_tensor("HA", [TOK, D], F32).ap()
    MOE = nc.dram_tensor("MOE", [TOK, D], F32).ap()
    dbg_pairs = []
    if dbg:
        for nm, t in [("XN", XN), ("XST_loc", XST_loc), ("RST_loc", RST_loc),
                      ("CT_loc", CT_loc), ("DBT_loc", DBT_loc),
                      ("Y3T_loc", Y3T_loc), ("HA", HA), ("MOE", MOE)]:
            d = nc.dram_tensor(nm + "_dbg", list(t.shape), t.dtype,
                               kind="ExternalOutput").ap()
            dbg_pairs.append((t, d))

    ES0 = contextlib.ExitStack()
    with ES0:
        s_sl = ES0.enter_context(nc.semaphore("sl"))
        s_sg = ES0.enter_context(nc.semaphore("sg"))
        s_sh = ES0.enter_context(nc.semaphore("sh"))
        s_sm = ES0.enter_context(nc.semaphore("sm"))
        s_se = ES0.enter_context(nc.semaphore("se"))
        s_sv = ES0.enter_context(nc.semaphore("sv"))
        s_sp = ES0.enter_context(nc.semaphore("sp"))
        s_sc = ES0.enter_context(nc.semaphore("sc"))
        scid = ES0.enter_context(nc.sbuf_tensor("scid", [1, 1], I32))
        ident = ES0.enter_context(nc.sbuf_tensor("ident", [128, 128], BF16))
        identf = ES0.enter_context(nc.sbuf_tensor("identf", [128, 128], F32))
        ceps = ES0.enter_context(nc.sbuf_tensor("ceps", [128, 1], F32))
        block = ES0.enter_context(nc.Block())
        sems = dict(sl=s_sl, sg=s_sg, sh=s_sh, sm=s_sm, se=s_se, sv=s_sv,
                    sp=s_sp, sc=s_sc)
        P = Prog(nc, block, sems)

        def _stop_here():
            w = [P.tok("sl"), P.tok("sg"), P.tok("sv"), P.tok("se"),
                 P.tok("sm"), P.tok("sp")]
            for (t, d) in dbg_pairs:
                P.dma("gpsimd", d, t, waits=w)
            P.step("sync", lambda s: s.nop(), waits=[P.tok("sl"), P.tok("sg"),
                                                     P.tok("sv"), P.tok("se"),
                                                     P.tok("sm"), P.tok("sp")])
            P.flush()

        def _pbw():
            return [P.tok(s) for s in ("sl", "sg", "sv", "se", "sm", "sp")]


        ESP0 = contextlib.ExitStack()
        with ESP0:
            iot = ESP0.enter_context(nc.sbuf_tensor("iot", [128, 128], F32))
            pid = ESP0.enter_context(nc.sbuf_tensor("pid", [128, 1], F32))
            idf = ESP0.enter_context(nc.sbuf_tensor("idf", [128, 128], F32))
            t_cid = P.dma("sync", scid[:], cid_in)
            P.step("gpsimd", lambda g: g.iota(iot[:], pattern=[[1, 128]], base=0,
                   channel_multiplier=0, allow_small_or_imprecise_dtypes=True))
            t_iota = P.step("gpsimd", lambda g: g.iota(pid[:], pattern=[[0, 1]], base=0,
                            channel_multiplier=1, allow_small_or_imprecise_dtypes=True),
                            inc="sp")
            P.step("vector", lambda v: v.tensor_scalar(idf[:], iot[:], pid[:, 0:1],
                   None, ALU.is_equal), waits=[t_iota])
            t_eps = P.step("vector", lambda v: v.memset(ceps[:], EPS), inc="sv")
            t_id = P.step("vector", lambda v: v.tensor_copy(ident[:], idf[:]), inc="sv")
            t_idf = P.step("vector", lambda v: v.tensor_copy(identf[:], idf[:]), inc="sv")
            P.flush()

        NTQ = 4
        TQ = TOK // NTQ
        a_out_toks = []
        ESA = contextlib.ExitStack()
        with ESA:
            a_xt = ESA.enter_context(nc.sbuf_tensor("a_xt", [128, D], F32))
            a_xc = ESA.enter_context(nc.sbuf_tensor("a_xc", [128, D], F32))
            a_tmp = ESA.enter_context(nc.sbuf_tensor("a_tmp", [128, D], F32))
            a_st = ESA.enter_context(nc.sbuf_tensor("a_st", [128, 8], F32))
            a_xnb = ESA.enter_context(nc.sbuf_tensor("a_xnb", [128, D], BF16))
            a_xnt = ESA.enter_context(nc.sbuf_tensor("a_xnt", [128, 8 * TQ], BF16))
            a_wk = ESA.enter_context(nc.sbuf_tensor("a_wk", [128, 2 * 16 * 128], BF16))
            a_bia = ESA.enter_context(nc.sbuf_tensor("a_bia", [128, 49], F32))
            a_xsq = ESA.enter_context(nc.sbuf_tensor("a_xsq", [128, 16 * TQ], BF16))
            a_ev = ESA.enter_context(nc.sbuf_tensor("a_ev", [128, 2 * 512], F32))
            a_evb = ESA.enter_context(nc.sbuf_tensor("a_evb", [128, 2 * 512], BF16))
            a_ps = ESA.enter_context(nc.psum_tensor("a_ps", [128, 2 * 512], F32))
            a_pt = ESA.enter_context(nc.psum_tensor("a_pt", [128, 2 * 1024], BF16))
            pba = _pbw()
            t_bz = P.step("vector", lambda v: v.memset(a_tmp[:, 0:128], 0.0),
                          waits=pba, inc="sv")
            t_bd1 = P.dma("sync", a_tmp[0:32, 0:128],
                          bi.rearrange("(a p) o -> a (p o)", p=128), waits=[t_bz])
            t_btp1 = P.step("tensor", lambda t: t.transpose(
                a_ps[:, 0:128].bitcast(F32) if False else a_ps[:, 0:128],
                a_tmp[:, 0:128], identf[:]), waits=[t_bd1, t_idf], inc="sm")
            t_bia1 = P.step("vector", lambda v: v.tensor_copy(a_bia[:, 0:32],
                            a_ps[:, 0:32]), waits=[t_btp1], inc="sv")
            t_bz2 = P.step("vector", lambda v: v.memset(a_tmp[:, 0:128], 0.0),
                           waits=[t_btp1], inc="sv")
            t_bd2 = P.dma("sync", a_tmp[0:17, 0:128],
                          bx.rearrange("(a p) o -> a (p o)", p=128), waits=[t_bz2])
            t_btp2 = P.step("tensor", lambda t: t.transpose(
                a_ps[:, 0:128], a_tmp[:, 0:128], identf[:]), waits=[t_bd2], inc="sm")
            t_bia2 = P.step("vector", lambda v: v.tensor_copy(a_bia[:, 32:49],
                            a_ps[:, 0:17]), waits=[t_btp2], inc="sv")

            t_lnlast = None
            t_xnout_prev = None
            t_tp_ev = [None, None]
            t_mm_ev = [None, None]
            t_wk = [None, None]
            t_wk_mm = [None, None]
            t_evb_out = [None, None]
            t_ev_out = [None, None]
            t_xsq_ready = None

            for tq in range(NTQ):
                for tt in range(8):
                    t0 = tq * TQ + tt * 128
                    t_x = P.dma("sync", a_xt[:], xloc[t0:t0 + 128, :],
                                waits=[t_lnlast] + (pba if tq == 0 and tt == 0 else []))
                    emit_ln(P, None, a_xt[:], a_xc[:], a_tmp[:], a_st, ceps,
                            [t_x, t_xnout_prev, t_eps])
                    t_lnlast = P.step("vector", lambda v: v.tensor_copy(a_xnb[:], a_xc[:]),
                                      inc="sv")
                    t_xnout_prev = P.dma("sync", XN[t0:t0 + 128, :], a_xc[:],
                                         waits=[t_lnlast])
                    a_out_toks.append(t_xnout_prev)
                    for dc in range(8):
                        bk = (tt * 8 + dc) % 2
                        t_tp = P.step("tensor", lambda t, dc=dc, bk=bk: t.transpose(
                            a_pt[:, bk * 1024:bk * 1024 + 128],
                            a_xnb[:, dc * 128:(dc + 1) * 128], ident[:]),
                            waits=[t_lnlast, t_id, t_tp_ev[bk]], inc="sm")
                        t_tp_ev[bk] = P.step("vector", lambda v, dc=dc, tt=tt, bk=bk:
                                             v.tensor_copy(
                                                 a_xnt[:, dc * TQ + tt * 128:dc * TQ + tt * 128 + 128],
                                                 a_pt[:, bk * 1024:bk * 1024 + 128]),
                                             waits=[t_tp], inc="sv")
                t_xnt_ready = t_tp_ev[(7 * 8 + 7) % 2]
                t_xnt_ready2 = t_tp_ev[(7 * 8 + 6) % 2]

                t_xst_dma_prev = a_out_toks[-1] if tq > 0 else None
                for fc in range(32):
                    wb = fc % 2
                    t_wl = None
                    for k in range(8):
                        t_wl = P.dma("gpsimd",
                                     a_wk[:, wb * 2048 + k * 128:wb * 2048 + (k + 1) * 128],
                                     wi[k * 128:(k + 1) * 128, fc * 128:(fc + 1) * 128],
                                     waits=[t_wk_mm[wb]] + (pba if tq == 0 and fc < 2 else []))
                    t_wk[wb] = t_wl
                    for F in range(2):
                        pb = (fc * 2 + F) % 2
                        t_mm = None
                        for k in range(8):
                            t_mm = P.step("tensor", lambda t, k=k, wb=wb, pb=pb, F=F:
                                          t.matmul(a_ps[:, pb * 512:(pb + 1) * 512],
                                                   a_wk[:, wb * 2048 + k * 128:wb * 2048 + (k + 1) * 128],
                                                   a_xnt[:, k * TQ + F * 512:k * TQ + F * 512 + 512],
                                                   start=(k == 0), stop=(k == 7)),
                                          waits=[t_wk[wb], t_xnt_ready, t_xnt_ready2,
                                                 t_mm_ev[pb]],
                                          inc="sm" if k == 7 else None)
                        t_wk_mm[wb] = t_mm
                        if fc < 16:
                            t_ev = P.step("scalar", lambda sc, fc=fc, F=F, pb=pb:
                                          sc.activation(
                                              out=a_xsq[:, fc * TQ + F * 512:fc * TQ + F * 512 + 512],
                                              in_=a_ps[:, pb * 512:(pb + 1) * 512],
                                              func=AF.Silu, bias=a_bia[:, fc:fc + 1]),
                                          waits=[t_mm, t_bia1, t_xst_dma_prev], inc="se")
                            t_mm_ev[pb] = t_ev
                            col = tq * TQ + F * 512
                            a_out_toks.append(P.dma("sync",
                                XST_loc[fc * 128:(fc + 1) * 128, col:col + 512],
                                a_xsq[:, fc * TQ + F * 512:fc * TQ + F * 512 + 512],
                                waits=[t_ev]))
                        else:
                            eb = F
                            t_ev = P.step("scalar", lambda sc, fc=fc, F=F, pb=pb, eb=eb:
                                          sc.activation(
                                              out=a_evb[:, eb * 512:(eb + 1) * 512],
                                              in_=a_ps[:, pb * 512:(pb + 1) * 512],
                                              func=AF.Silu, bias=a_bia[:, fc:fc + 1]),
                                          waits=[t_mm, t_bia1, t_evb_out[eb]], inc="se")
                            t_mm_ev[pb] = t_ev
                            col = tq * TQ + F * 512
                            t_o = P.dma("sync",
                                        RST_loc[(fc - 16) * 128:(fc - 15) * 128, col:col + 512],
                                        a_evb[:, eb * 512:(eb + 1) * 512], waits=[t_ev])
                            t_evb_out[eb] = t_o
                            a_out_toks.append(t_o)
                t_xsq_ready = t_mm_ev[0]
                t_xsq_ready2 = t_mm_ev[1]

                for f2 in range(17):
                    wb = f2 % 2
                    t_wl = None
                    for k in range(16):
                        t_wl = P.dma("gpsimd",
                                     a_wk[:, wb * 2048 + k * 128:wb * 2048 + (k + 1) * 128],
                                     wx[k * 128:(k + 1) * 128, f2 * 128:(f2 + 1) * 128],
                                     waits=[t_wk_mm[wb]])
                    t_wk[wb] = t_wl
                    for F in range(2):
                        pb = (f2 * 2 + F) % 2
                        t_mm = None
                        for k in range(16):
                            t_mm = P.step("tensor", lambda t, k=k, wb=wb, pb=pb, F=F:
                                          t.matmul(a_ps[:, pb * 512:(pb + 1) * 512],
                                                   a_wk[:, wb * 2048 + k * 128:wb * 2048 + (k + 1) * 128],
                                                   a_xsq[:, k * TQ + F * 512:k * TQ + F * 512 + 512],
                                                   start=(k == 0), stop=(k == 15)),
                                          waits=[t_wk[wb], t_xsq_ready, t_xsq_ready2,
                                                 t_mm_ev[pb]],
                                          inc="sm" if k == 15 else None)
                        t_wk_mm[wb] = t_mm
                        col = tq * TQ + F * 512
                        if f2 == 0:
                            eb = F
                            t_ev = P.step("vector", lambda v, pb=pb, eb=eb:
                                          v.tensor_scalar(a_ev[:, eb * 512:(eb + 1) * 512],
                                                          a_ps[:, pb * 512:(pb + 1) * 512],
                                                          a_bia[:, 32:33], None, ALU.add),
                                          waits=[t_mm, t_bia2, t_ev_out[eb]], inc="sv")
                            t_mm_ev[pb] = t_ev
                            t_o = P.dma("sync", DBT_loc[:, col:col + 512],
                                        a_ev[:, eb * 512:(eb + 1) * 512], waits=[t_ev])
                            t_ev_out[eb] = t_o
                            a_out_toks.append(t_o)
                        else:
                            eb = F
                            t_ev = P.step("vector", lambda v, f2=f2, pb=pb, eb=eb:
                                          v.tensor_scalar(a_evb[:, eb * 512:(eb + 1) * 512],
                                                          a_ps[:, pb * 512:(pb + 1) * 512],
                                                          a_bia[:, 32 + f2:33 + f2], None, ALU.add),
                                          waits=[t_mm, t_bia2, t_evb_out[eb]], inc="sv")
                            t_mm_ev[pb] = t_ev
                            t_o = P.dma("sync", CT_loc[(f2 - 1) * 128:f2 * 128, col:col + 512],
                                        a_evb[:, eb * 512:(eb + 1) * 512], waits=[t_ev])
                            t_evb_out[eb] = t_o
                            a_out_toks.append(t_o)
            P.flush()

        if stop_after == "A":
            _stop_here()
            return nc
        all_a = [P.tok("sl"), P.tok("sg")]
        t_ag_dbt = P.step("gpsimd", lambda g: g.collective_compute(
            "AllGather", ALU.bypass, replica_groups=[list(range(NC))],
            ins=[DBT_loc], outs=[DBT_g]), waits=all_a, inc="sc")
        t_ag_xst = P.step("gpsimd", lambda g: g.collective_compute(
            "AllGather", ALU.bypass, replica_groups=[list(range(NC))],
            ins=[XST_loc], outs=[XST_g]), inc="sc")
        t_ag_ct = P.step("gpsimd", lambda g: g.collective_compute(
            "AllGather", ALU.bypass, replica_groups=[list(range(NC))],
            ins=[CT_loc], outs=[CT_g]), inc="sc")
        t_ag_rst = P.step("gpsimd", lambda g: g.collective_compute(
            "AllGather", ALU.bypass, replica_groups=[list(range(NC))],
            ins=[RST_loc], outs=[RST_g]), inc="sc")
        P.flush()

        if stop_after == "AG":
            _stop_here()
            return nc
        TB = 4096
        y3_toks = []
        ESB = contextlib.ExitStack()
        with ESB:
            b_A = ESB.enter_context(nc.sbuf_tensor("b_A", [128, 2 * NST], F32))
            b_dp = ESB.enter_context(nc.sbuf_tensor("b_dp", [128, 2], F32))
            b_bdt = ESB.enter_context(nc.sbuf_tensor("b_bdt", [128, 2], F32))
            b_wdt = ESB.enter_context(nc.sbuf_tensor("b_wdt", [64, DSH], F32))
            b_rhs = ESB.enter_context(nc.sbuf_tensor("b_rhs", [64, 2 * 512], F32))
            b_del = ESB.enter_context(nc.sbuf_tensor("b_del", [128, 2 * TB], F32))
            b_w = ESB.enter_context(nc.sbuf_tensor("b_w", [128, 2 * TB], BF16))
            b_xs = ESB.enter_context(nc.sbuf_tensor("b_xs", [128, 2 * TB], BF16))
            b_acc = ESB.enter_context(nc.sbuf_tensor("b_acc", [128, 2 * TB], F32))
            b_dA = ESB.enter_context(nc.sbuf_tensor("b_dA", [128, 2 * TB], F32))
            b_u = ESB.enter_context(nc.sbuf_tensor("b_u", [128, TB], F32))
            b_h = ESB.enter_context(nc.sbuf_tensor("b_h", [128, TB], F32))
            b_rep = ESB.enter_context(nc.sbuf_tensor("b_rep", [128, 2 * TB], BF16))
            b_ub = ESB.enter_context(nc.sbuf_tensor("b_ub", [128, 2 * TB], BF16))
            b_car = ESB.enter_context(nc.sbuf_tensor("b_car", [128, 2 * NST], F32))
            b_tl = ESB.enter_context(nc.sbuf_tensor("b_tl", [128, TB], BF16))
            b_ps = ESB.enter_context(nc.psum_tensor("b_ps", [128, 2 * 512], F32))
            pbb = _pbw()
            tb1 = P.dma("sync", b_A[:, 0:NST], alogs[0:128, :], waits=pbb)
            tb2 = P.dma("sync", b_A[:, NST:2 * NST], alogs[128:256, :], waits=pbb)
            tb3 = P.dma("sync", b_dp[:, 0:1], dps[0:128, :])
            tb4 = P.dma("sync", b_dp[:, 1:2], dps[128:256, :])
            tb5 = P.dma("sync", b_bdt[:, 0:1], bdts[0:128, :])
            tb6 = P.dma("sync", b_bdt[:, 1:2], bdts[128:256, :])
            tb3b = [P.step("gpsimd", lambda g: g.nop(), waits=pbb)]
            tb7 = P.dma("sync", b_wdt[:, :], wdts)
            P.step("vector", lambda v: v.nop(), waits=pbb)
            P.step("scalar", lambda sc: sc.nop(), waits=pbb)
            P.step("tensor", lambda t: t.nop(), waits=pbb)
            t_ae = P.step("scalar", lambda sc: sc.activation(out=b_A[:, :], in_=b_A[:, :],
                          func=AF.Exp), waits=[tb1, tb2], inc="se")
            t_an = P.step("vector", lambda v: v.tensor_scalar(b_A[:, :], b_A[:, :], -1.0,
                          None, ALU.mult), waits=[t_ae], inc="sv")

            cidv = {}

            def mk_cid(engname):
                def fn(e):
                    r = e.alloc_register("cidr_" + engname)
                    e.reg_load(r, scid[0:1, 0:1])
                    cidv[engname] = e.snap(r, min_val=0, max_val=NC - 1)
                    return _nopwait(e)
                return fn
            P.step("sync", mk_cid("sync"), waits=[t_cid])
            for ii in range(NC):
                t_cp_xst = P.step("sync", lambda s, ii=ii: s.dma_start(
                    out=XST_my[ii * DSH:(ii + 1) * DSH, :],
                    in_=XST_g[bass.ds(cidv["sync"] * DSH + ii * DI, DSH), :]),
                    waits=[t_ag_xst], inc="sl", amt=16)
            for ii in range(NC):
                t_cp_ct = P.step("sync", lambda s, ii=ii: s.dma_start(
                    out=CT_my[ii * DSH:(ii + 1) * DSH, :],
                    in_=CT_g[bass.ds(cidv["sync"] * DSH + ii * DI, DSH), :]),
                    waits=[t_ag_ct], inc="sl", amt=16)
            for ii in range(NC):
                t_cp_rst = P.step("sync", lambda s, ii=ii: s.dma_start(
                    out=RST_my[ii * DSH:(ii + 1) * DSH, :],
                    in_=RST_g[bass.ds(cidv["sync"] * DSH + ii * DI, DSH), :]),
                    waits=[t_ag_rst], inc="sl", amt=16)

            t_del_mm_ev = [None, None]
            t_rhs_mm = [None, None]

            t_xs_read = None
            t_w_read = None
            t_tail_prev = None
            t_rep_read = [None, None]
            t_rep_dma = [None, None]
            t_y3_out = None

            for bb in range(B):
                for hh in range(2):
                    iblk = [4 * hh + j for j in range(4)]
                    t_del_done = [None, None]
                    t_w_done = [None, None]
                    t_xs_dma = [None, None]
                    for dg in range(2):
                        for j, ib in enumerate(iblk):
                            t_xs_dma[dg] = P.step("sync", lambda s, ib=ib, j=j, dg=dg, bb=bb:
                                s.dma_start(out=b_xs[:, dg * TB + j * 1024:dg * TB + j * 1024 + 1024],
                                            in_=XST_my[ib * DSH + dg * 128:ib * DSH + dg * 128 + 128,
                                                       bb * 1024:(bb + 1) * 1024]),
                                waits=[t_cp_xst, t_xs_read], inc="sl", amt=16)
                        for Fc in range(8):
                            ib = iblk[Fc // 2]
                            off = (Fc % 2) * 512
                            rb = Fc % 2
                            t_r = P.dma("sync", b_rhs[:, rb * 512:(rb + 1) * 512],
                                        DBT_g[128 * ib:128 * ib + 64,
                                              bb * 1024 + off:bb * 1024 + off + 512],
                                        waits=[t_ag_dbt, t_rhs_mm[rb]])
                            pb = Fc % 2
                            t_mm = P.step("tensor", lambda t, dg=dg, rb=rb, pb=pb:
                                          t.matmul(b_ps[:, pb * 512:(pb + 1) * 512],
                                                   b_wdt[:, dg * 128:(dg + 1) * 128],
                                                   b_rhs[:, rb * 512:(rb + 1) * 512],
                                                   start=True, stop=True),
                                          waits=[t_r, tb7, t_del_mm_ev[pb]], inc="sm")
                            t_rhs_mm[rb] = t_mm
                            t_ev = P.step("vector", lambda v, dg=dg, Fc=Fc, pb=pb:
                                          v.tensor_scalar(b_del[:, dg * TB + Fc * 512:dg * TB + (Fc + 1) * 512],
                                                          b_ps[:, pb * 512:(pb + 1) * 512],
                                                          b_bdt[:, dg:dg + 1], None, ALU.add),
                                          waits=[t_mm, tb5, tb6, t_w_read], inc="sv")
                            t_del_mm_ev[pb] = t_ev
                        dsl = (lambda dg=dg: b_del[:, dg * TB:(dg + 1) * TB])()
                        P.step("scalar", lambda sc, dsl=dsl: sc.activation(out=b_u[:],
                               in_=dsl, func=AF.Abs), waits=[t_del_mm_ev[0], t_del_mm_ev[1], t_tail_prev])
                        P.step("scalar", lambda sc: sc.activation(out=b_u[:], in_=b_u[:, 0:TB],
                               func=AF.Exp, scale=-1.0))
                        t_sp1 = P.step("scalar", lambda sc: sc.activation(out=b_u[:],
                                       in_=b_u[:], func=AF.Ln, bias=1.0), inc="se")
                        P.step("vector", lambda v, dsl=dsl: v.tensor_relu(b_h[:], dsl),
                               waits=[t_sp1])
                        t_spd = P.step("vector", lambda v, dsl=dsl: v.tensor_add(dsl, b_h[:], b_u[:]),
                                       inc="sv")
                        t_del_done[dg] = t_spd
                        t_w = P.step("vector", lambda v, dg=dg, dsl=dsl: v.tensor_mul(
                            b_w[:, dg * TB:(dg + 1) * TB], dsl,
                            b_xs[:, dg * TB:(dg + 1) * TB]),
                            waits=[t_xs_dma[dg]], inc="sv")
                        t_w_done[dg] = t_w
                        P.step("vector", lambda v, dg=dg: v.memset(
                            b_acc[:, dg * TB:(dg + 1) * TB], 0.0))

                    t_scan = None
                    t_scan_dg = [None, None]
                    t_dA_prev = None
                    for n in range(NST):
                        rb = n % 2
                        t_rp = None
                        for j, ib in enumerate(iblk):
                            t_rp = P.step("gpsimd", lambda g, ib=ib, j=j, n=n, bb=bb, rb=rb:
                                g.dma_start(out=b_rep[:, rb * TB + j * 1024:rb * TB + j * 1024 + 1024],
                                            in_=DBT_g[128 * ib + 64 + n:128 * ib + 64 + n + 1,
                                                      bb * 1024:(bb + 1) * 1024].partition_broadcast(128)),
                                waits=[t_ag_dbt, t_rep_read[rb]], inc="sg", amt=16)
                        t_rep_dma[rb] = t_rp
                        for dg in range(2):
                            db = (n * 2 + dg) % 2
                            t_dA = P.step("scalar", lambda sc, dg=dg, n=n, db=db: sc.activation(
                                out=b_dA[:, db * TB:(db + 1) * TB],
                                in_=b_del[:, dg * TB:(dg + 1) * TB],
                                func=AF.Exp, scale=b_A[:, dg * NST + n:dg * NST + n + 1]),
                                waits=[t_del_done[dg], t_an, t_scan_dg[dg]], inc="se")
                            t_u = P.step("vector", lambda v, dg=dg, rb=rb: v.tensor_mul(
                                b_ub[:, dg * TB:(dg + 1) * TB],
                                b_w[:, dg * TB:(dg + 1) * TB],
                                b_rep[:, rb * TB:(rb + 1) * TB]),
                                waits=[t_rp], inc="sv")
                            t_rep_read[rb] = t_u
                            if hh == 0:
                                t_scan = P.step("vector", lambda v, dg=dg, db=db: v.tensor_tensor_scan(
                                    b_h[:], b_dA[:, db * TB:(db + 1) * TB],
                                    b_ub[:, dg * TB:(dg + 1) * TB],
                                    0.0, ALU.mult, ALU.add),
                                    waits=[t_dA], inc="sv")
                            else:
                                t_scan = P.step("vector", lambda v, dg=dg, n=n, db=db:
                                    v.tensor_tensor_scan(
                                        b_h[:], b_dA[:, db * TB:(db + 1) * TB],
                                        b_ub[:, dg * TB:(dg + 1) * TB],
                                        b_car[:, dg * NST + n:dg * NST + n + 1],
                                        ALU.mult, ALU.add),
                                    waits=[t_dA], inc="sv")
                            t_scan_dg[dg] = t_scan
                            P.step("vector", lambda v, dg=dg: v.tensor_add(
                                b_acc[:, dg * TB:(dg + 1) * TB],
                                b_acc[:, dg * TB:(dg + 1) * TB], b_h[:]))
                            if hh == 0:
                                P.step("vector", lambda v, dg=dg, n=n:
                                    v.tensor_copy(b_car[:, dg * NST + n:dg * NST + n + 1],
                                                  b_h[:, TB - 1:TB]))
                            t_dA_prev = t_dA

                    for dg in range(2):
                        t_ct = None
                        for j, ib in enumerate(iblk):
                            t_ct = P.step("sync", lambda s, ib=ib, j=j, dg=dg, bb=bb:
                                s.dma_start(out=b_tl[:, j * 1024:(j + 1) * 1024],
                                            in_=CT_my[ib * DSH + dg * 128:ib * DSH + dg * 128 + 128,
                                                      bb * 1024:(bb + 1) * 1024]),
                                waits=[t_cp_ct, t_tail_prev], inc="sl", amt=16)
                        t_y1 = P.step("vector", lambda v, dg=dg: v.tensor_mul(
                            b_u[:], b_tl[:], b_acc[:, dg * TB:(dg + 1) * TB]),
                            waits=[t_ct], inc="sv")
                        P.step("vector", lambda v, dg=dg: v.scalar_tensor_tensor(
                            b_u[:], b_xs[:, dg * TB:(dg + 1) * TB], b_dp[:, dg:dg + 1],
                            b_u[:], ALU.mult, ALU.add), waits=[tb3, tb4])
                        t_rst = None
                        for j, ib in enumerate(iblk):
                            t_rst = P.step("sync", lambda s, ib=ib, j=j, dg=dg, bb=bb:
                                s.dma_start(out=b_tl[:, j * 1024:(j + 1) * 1024],
                                            in_=RST_my[ib * DSH + dg * 128:ib * DSH + dg * 128 + 128,
                                                       bb * 1024:(bb + 1) * 1024]),
                                waits=[t_cp_rst, t_y1], inc="sl", amt=16)
                        t_y3 = P.step("vector", lambda v: v.tensor_mul(b_tl[:], b_u[:], b_tl[:]),
                                      waits=[t_rst], inc="sv")
                        t_out_prev = None
                        for j, ib in enumerate(iblk):
                            t_out_prev = P.dma("sync",
                                Y3T_loc[dg * 128:(dg + 1) * 128,
                                        ib * TOK + bb * 1024:ib * TOK + (bb + 1) * 1024],
                                b_tl[:, j * 1024:(j + 1) * 1024], waits=[t_y3])
                            y3_toks.append(t_out_prev)
                        t_tail_prev = t_out_prev
                        t_xs_read = t_tail_prev
                        t_w_read = t_scan
            P.flush()

        if stop_after == "B":
            _stop_here()
            return nc
        t_ag_y3 = P.step("gpsimd", lambda g: g.collective_compute(
            "AllGather", ALU.bypass, replica_groups=[list(range(NC))],
            ins=[Y3T_loc], outs=[Y3T_g]), waits=[P.tok("sl")], inc="sc")
        t_cp_y3 = None
        for ii in range(4):
            t_cp_y3 = P.step("sync", lambda s, ii=ii: s.dma_start(
                out=Y3T_my[ii * 512:(ii + 1) * 512, :],
                in_=Y3T_g[ii * 512:(ii + 1) * 512, bass.ds(cidv["sync"] * TOK, TOK)]),
                waits=[t_ag_y3], inc="sl", amt=16)
        P.flush()

        if stop_after == "BG":
            _stop_here()
            return nc
        out_toks = []
        ESC = contextlib.ExitStack()
        with ESC:
            c_l2T = ESC.enter_context(nc.sbuf_tensor("c_l2T", [128, 8 * TOK], BF16))
            c_wexp = ESC.enter_context(nc.sbuf_tensor("c_wexp", [128, 32 * 8], F32))
            c_rep7 = ESC.enter_context(nc.sbuf_tensor("c_rep7", [128, 7 * D], F32))
            c_st = ESC.enter_context(nc.sbuf_tensor("c_st", [128, 8], F32))
            pbc = _pbw()
            P.step("vector", lambda v: v.nop(), waits=pbc)
            P.step("scalar", lambda sc: sc.nop(), waits=pbc)
            P.step("tensor", lambda t: t.nop(), waits=pbc)
            reps = [l1w, l1b, l2w, l2b, gam, bet, bo]
            t_rep = None
            for i, src in enumerate(reps):
                t_rep = P.dma("sync", c_rep7[:, i * D:(i + 1) * D],
                              src.partition_broadcast(128), waits=pbc)
            RW = {"l1w": 0, "l1b": 1, "l2w": 2, "l2b": 3, "gam": 4, "bet": 5, "bo": 6}

            def rsl(name):
                i = RW[name]
                return c_rep7[:, i * D:(i + 1) * D]

            ESC1 = contextlib.ExitStack()
            with ESC1:
                c_qkvT = ESC1.enter_context(nc.sbuf_tensor("c_qkvT", [128, 8 * 1536], BF16))
                c_projT = ESC1.enter_context(nc.sbuf_tensor("c_projT", [128, 8 * D], BF16))
                c_rT = ESC1.enter_context(nc.sbuf_tensor("c_rT", [128, 8 * 8], BF16))
                c_wtmp = ESC1.enter_context(nc.sbuf_tensor("c_wtmp", [128, D], BF16))
                c_rtmp = ESC1.enter_context(nc.sbuf_tensor("c_rtmp", [128, D], BF16))
                c_xn4 = ESC1.enter_context(nc.sbuf_tensor("c_xn4", [128, 4 * D], F32))
                c_xc = ESC1.enter_context(nc.sbuf_tensor("c_xc", [128, D], F32))
                c_t1 = ESC1.enter_context(nc.sbuf_tensor("c_t1", [128, D], F32))
                c_t2 = ESC1.enter_context(nc.sbuf_tensor("c_t2", [128, D], F32))
                c_bf = ESC1.enter_context(nc.sbuf_tensor("c_bf", [128, D], BF16))
                c_lT = ESC1.enter_context(nc.sbuf_tensor("c_lT", [128, 8 * 128], BF16))
                c_q = ESC1.enter_context(nc.sbuf_tensor("c_q", [128, 4 * D], BF16))
                c_k = ESC1.enter_context(nc.sbuf_tensor("c_k", [128, 4 * 256], BF16))
                c_v = ESC1.enter_context(nc.sbuf_tensor("c_v", [128, 4 * 256], F32))
                c_pr = ESC1.enter_context(nc.sbuf_tensor("c_pr", [128, 128], BF16))
                c_sc = ESC1.enter_context(nc.sbuf_tensor("c_sc", [128, 4 * 64], F32))
                c_sm = ESC1.enter_context(nc.sbuf_tensor("c_sm", [128, 48], F32))
                c_me = ESC1.enter_context(nc.sbuf_tensor("c_me", [128, 64], F32))
                c_e = ESC1.enter_context(nc.sbuf_tensor("c_e", [128, 64], F32))
                c_p = ESC1.enter_context(nc.sbuf_tensor("c_p", [128, 4 * 64], F32))
                c_at = ESC1.enter_context(nc.sbuf_tensor("c_at", [128, D], F32))
                c_ps3 = ESC1.enter_context(nc.psum_tensor("c_ps3", [128, 3 * 512], F32))
                c_pt = ESC1.enter_context(nc.psum_tensor("c_pt", [128, 2 * 1024], BF16))
                c_psr = ESC1.enter_context(nc.psum_tensor("c_psr", [128, 8], F32))
                t_tp_ev = [None, None]
                t_wt_dma = None
                for rc in range(12):
                    t_wt_dma = P.dma("gpsimd", c_wtmp[:], qkvw[rc * 128:(rc + 1) * 128, :],
                                     waits=[t_tp_ev[0], t_tp_ev[1]] + (pbc if rc == 0 else []))
                    for dc in range(8):
                        bk = dc % 2
                        t_tp = P.step("tensor", lambda t, dc=dc, bk=bk: t.transpose(
                            c_pt[:, bk * 1024:bk * 1024 + 128],
                            c_wtmp[:, dc * 128:(dc + 1) * 128], ident[:]),
                            waits=[t_wt_dma, t_id, t_tp_ev[bk]], inc="sm")
                        t_tp_ev[bk] = P.step("vector", lambda v, dc=dc, rc=rc, bk=bk:
                            v.tensor_copy(c_qkvT[:, dc * 1536 + rc * 128:dc * 1536 + (rc + 1) * 128],
                                          c_pt[:, bk * 1024:bk * 1024 + 128]),
                            waits=[t_tp], inc="sv")
                for rc in range(8):
                    t_wt_dma = P.dma("gpsimd", c_wtmp[:], projw[rc * 128:(rc + 1) * 128, :],
                                     waits=[t_tp_ev[0], t_tp_ev[1]])
                    for dc in range(8):
                        bk = dc % 2
                        t_tp = P.step("tensor", lambda t, dc=dc, bk=bk: t.transpose(
                            c_pt[:, bk * 1024:bk * 1024 + 128],
                            c_wtmp[:, dc * 128:(dc + 1) * 128], ident[:]),
                            waits=[t_wt_dma, t_tp_ev[bk]], inc="sm")
                        t_tp_ev[bk] = P.step("vector", lambda v, dc=dc, rc=rc, bk=bk:
                            v.tensor_copy(c_projT[:, dc * D + rc * 128:dc * D + rc * 128 + 128],
                                          c_pt[:, bk * 1024:bk * 1024 + 128]),
                            waits=[t_tp], inc="sv")
                t_rt0 = P.step("vector", lambda v: v.memset(c_rtmp[:], 0.0),
                               waits=[t_tp_ev[0], t_tp_ev[1]], inc="sv")
                t_rt_dma = P.dma("gpsimd", c_rtmp[0:8, :], routw, waits=[t_rt0])
                for dc in range(8):
                    bk = dc % 2
                    t_tp = P.step("tensor", lambda t, dc=dc, bk=bk: t.transpose(
                        c_pt[:, bk * 1024:bk * 1024 + 128],
                        c_rtmp[:, dc * 128:(dc + 1) * 128], ident[:]),
                        waits=[t_rt_dma, t_tp_ev[bk]], inc="sm")
                    t_tp_ev[bk] = P.step("vector", lambda v, dc=dc, bk=bk:
                        v.tensor_copy(c_rT[:, dc * 8:(dc + 1) * 8],
                                      c_pt[:, bk * 1024:bk * 1024 + 8]),
                        waits=[t_tp], inc="sv")

                t_sl_c = P.tok("sl")
                t_lT_mm = None
                t_ps3_ev = None
                t_psr_ev = None
                t_ha_out = None
                t_l2T_wr = [None]
                for lc in range(8):
                    t_qkv_ev = [None, None, None, None]
                    for bb in range(B):
                        t0 = bb * LC + lc * 128
                        t_xn = P.dma("sync", c_xn4[:, bb * D:(bb + 1) * D],
                                     XN[t0:t0 + 128, :], waits=[t_ha_out, t_sl_c])
                        emit_ln(P, None, c_xn4[:, bb * D:(bb + 1) * D], c_xc[:],
                                c_t1[:], c_st, ceps, [t_xn, t_rep, t_eps])
                        P.step("vector", lambda v: v.tensor_mul(c_t1[:], c_xc[:], rsl("l1w")))
                        P.step("vector", lambda v: v.tensor_add(c_t1[:], c_t1[:], rsl("l1b")))
                        t_l1 = P.step("vector", lambda v: v.tensor_copy(c_bf[:], c_t1[:]),
                                      inc="sv")
                        for dc in range(8):
                            bk = dc % 2
                            t_tp = P.step("tensor", lambda t, dc=dc, bk=bk: t.transpose(
                                c_pt[:, bk * 1024:bk * 1024 + 128],
                                c_bf[:, dc * 128:(dc + 1) * 128], ident[:]),
                                waits=[t_l1, t_tp_ev[bk], t_lT_mm], inc="sm")
                            t_tp_ev[bk] = P.step("vector", lambda v, dc=dc, bk=bk:
                                v.tensor_copy(c_lT[:, dc * 128:(dc + 1) * 128],
                                              c_pt[:, bk * 1024:bk * 1024 + 128]),
                                waits=[t_tp], inc="sv")
                        t_mm = None
                        for k in range(8):
                            for Fq in range(3):
                                t_mm = P.step("tensor", lambda t, k=k, Fq=Fq:
                                    t.matmul(c_ps3[:, Fq * 512:(Fq + 1) * 512],
                                             c_lT[:, k * 128:(k + 1) * 128],
                                             c_qkvT[:, k * 1536 + Fq * 512:k * 1536 + Fq * 512 + 512],
                                             start=(k == 0), stop=(k == 7)),
                                    waits=[t_tp_ev[0], t_tp_ev[1], t_ps3_ev],
                                    inc="sm" if (k == 7 and Fq == 2) else None)
                        t_lT_mm = t_mm
                        P.step("vector", lambda v, bb=bb: v.tensor_scalar(
                            c_q[:, bb * D:(bb + 1) * D], c_ps3[:, 0:1024], 0.125,
                            None, ALU.mult), waits=[t_mm])
                        P.step("vector", lambda v, bb=bb: v.tensor_copy(
                            c_k[:, bb * 256:(bb + 1) * 256], c_ps3[:, 1024:1280]))
                        t_qkv_ev[bb] = P.step("vector", lambda v, bb=bb: v.tensor_copy(
                            c_v[:, bb * 256:(bb + 1) * 256], c_ps3[:, 1280:1536]), inc="sv")
                        t_ps3_ev = t_qkv_ev[bb]

                    t_scdone = None
                    for s in range(B):
                        for tp in range(s + 1):
                            for h in range(16):
                                kv = h // 4
                                sl2 = h % 2
                                P.step("vector", lambda v, s=s, tp=tp, h=h, kv=kv, sl2=sl2:
                                       v.tensor_mul(c_pr[:, sl2 * 64:(sl2 + 1) * 64],
                                                    c_q[:, s * D + h * 64:s * D + h * 64 + 64],
                                                    c_k[:, tp * 256 + kv * 64:tp * 256 + kv * 64 + 64]),
                                       waits=[t_qkv_ev[s], t_qkv_ev[tp]])
                                if h > 0:
                                    po = (h - 1) % 2
                                    P.step("vector", lambda v, s=s, tp=tp, h=h, po=po:
                                           v.tensor_reduce(c_sc[:, s * 64 + (h - 1) * 4 + tp:s * 64 + (h - 1) * 4 + tp + 1],
                                                           c_pr[:, po * 64:(po + 1) * 64], AX.X, ALU.add))
                            P.step("vector", lambda v: v.tensor_copy(c_me[:, 0:8], c_sm[:, 0:8]))
                            P.step("vector", lambda v, s=s, tp=tp:
                                   v.tensor_reduce(c_sc[:, s * 64 + 15 * 4 + tp:s * 64 + 15 * 4 + tp + 1],
                                                   c_pr[:, 64:128], AX.X, ALU.add))
                    for s in range(B):
                        scs = (lambda s=s: c_sc[:, s * 64:(s + 1) * 64])()
                        gap2 = lambda v: v.tensor_copy(c_me[:, 0:16], c_sm[:, 16:32])
                        P.step("vector", lambda v, s=s, scs=scs: v.tensor_reduce(
                            c_sm[:, 0:16],
                            scs.rearrange("p (h t) -> p h t", t=4)[:, :, 0:s + 1],
                            AX.X, ALU.max))
                        P.step("vector", gap2)
                        for tp in range(s + 1):
                            P.step("vector", lambda v, tp=tp: v.tensor_copy(
                                c_me[:].rearrange("p (h t) -> p h t", t=4)[:, :, tp:tp + 1],
                                c_sm[:, 0:16].rearrange("p (h o) -> p h o", o=1)))
                        P.step("vector", lambda v: v.tensor_copy(c_sm[:, 16:32], c_sm[:, 0:16]))
                        t_sub = P.step("vector", lambda v, scs=scs: v.tensor_sub(
                            c_e[:], scs, c_me[:]), inc="sv")
                        t_exp = P.step("scalar", lambda sc: sc.activation(
                            out=c_e[:], in_=c_e[:], func=AF.Exp), waits=[t_sub], inc="se")
                        P.step("vector", lambda v, s=s: v.tensor_reduce(
                            c_sm[:, 16:32],
                            c_e[:].rearrange("p (h t) -> p h t", t=4)[:, :, 0:s + 1],
                            AX.X, ALU.add), waits=[t_exp])
                        P.step("vector", lambda v: v.tensor_copy(c_me[:, 0:16], c_sm[:, 0:16]))
                        P.step("vector", lambda v: v.reciprocal(c_sm[:, 32:48], c_sm[:, 16:32]))
                        P.step("vector", lambda v: v.tensor_copy(c_me[:, 0:16], c_sm[:, 0:16]))
                        for tp in range(s + 1):
                            P.step("vector", lambda v, s=s, tp=tp: v.tensor_mul(
                                c_p[:, s * 64:(s + 1) * 64].rearrange("p (h t) -> p h t", t=4)[:, :, tp:tp + 1],
                                c_e[:].rearrange("p (h t) -> p h t", t=4)[:, :, tp:tp + 1],
                                c_sm[:, 32:48].rearrange("p (h o) -> p h o", o=1)))
                    t_probs = P.tok("sv")

                    for s in range(B):
                        t0 = s * LC + lc * 128
                        for tp in range(s + 1):
                            for h in range(16):
                                kv = h // 4
                                if tp == 0:
                                    P.step("vector", lambda v, s=s, tp=tp, h=h, kv=kv:
                                        v.tensor_scalar(c_at[:, h * 64:(h + 1) * 64],
                                                        c_v[:, tp * 256 + kv * 64:tp * 256 + kv * 64 + 64],
                                                        c_p[:, s * 64 + h * 4 + tp:s * 64 + h * 4 + tp + 1],
                                                        None, ALU.mult),
                                        waits=[t_probs])
                                else:
                                    P.step("vector", lambda v, s=s, tp=tp, h=h, kv=kv:
                                        v.scalar_tensor_tensor(c_at[:, h * 64:(h + 1) * 64],
                                                               c_v[:, tp * 256 + kv * 64:tp * 256 + kv * 64 + 64],
                                                               c_p[:, s * 64 + h * 4 + tp:s * 64 + h * 4 + tp + 1],
                                                               c_at[:, h * 64:(h + 1) * 64],
                                                               ALU.mult, ALU.add))
                        t_at = P.step("vector", lambda v: v.tensor_copy(c_bf[:], c_at[:]),
                                      inc="sv")
                        for dc in range(8):
                            bk = dc % 2
                            t_tp = P.step("tensor", lambda t, dc=dc, bk=bk: t.transpose(
                                c_pt[:, bk * 1024:bk * 1024 + 128],
                                c_bf[:, dc * 128:(dc + 1) * 128], ident[:]),
                                waits=[t_at, t_tp_ev[bk], t_lT_mm], inc="sm")
                            t_tp_ev[bk] = P.step("vector", lambda v, dc=dc, bk=bk:
                                v.tensor_copy(c_lT[:, dc * 128:(dc + 1) * 128],
                                              c_pt[:, bk * 1024:bk * 1024 + 128]),
                                waits=[t_tp], inc="sv")
                        t_mm = None
                        for k in range(8):
                            for Fp in range(2):
                                t_mm = P.step("tensor", lambda t, k=k, Fp=Fp:
                                    t.matmul(c_ps3[:, Fp * 512:(Fp + 1) * 512],
                                             c_lT[:, k * 128:(k + 1) * 128],
                                             c_projT[:, k * D + Fp * 512:k * D + Fp * 512 + 512],
                                             start=(k == 0), stop=(k == 7)),
                                    waits=[t_tp_ev[0], t_tp_ev[1], t_ps3_ev],
                                    inc="sm" if (k == 7 and Fp == 1) else None)
                        t_lT_mm = t_mm
                        t_ha = P.step("vector", lambda v, s=s: v.tensor_add(
                            c_t2[:], c_ps3[:, 0:1024], c_xn4[:, s * D:(s + 1) * D]),
                            waits=[t_mm], inc="sv")
                        t_ps3_ev = t_ha
                        t_ha_out = P.dma("sync", HA[t0:t0 + 128, :], c_t2[:], waits=[t_ha])
                        out_toks.append(t_ha_out)
                        emit_ln(P, None, c_t2[:], c_xc[:], c_t1[:], c_st, ceps, [t_eps])
                        P.step("vector", lambda v: v.tensor_mul(c_t1[:], c_xc[:], rsl("l2w")))
                        P.step("vector", lambda v: v.tensor_add(c_t1[:], c_t1[:], rsl("l2b")))
                        t_l2 = P.step("vector", lambda v: v.tensor_copy(c_bf[:], c_t1[:]),
                                      inc="sv")
                        tau0 = s * LC + lc * 128
                        for dc in range(8):
                            bk = dc % 2
                            t_tp = P.step("tensor", lambda t, dc=dc, bk=bk: t.transpose(
                                c_pt[:, bk * 1024:bk * 1024 + 128],
                                c_bf[:, dc * 128:(dc + 1) * 128], ident[:]),
                                waits=[t_l2, t_tp_ev[bk], t_lT_mm], inc="sm")
                            t_tp_ev[bk] = P.step("vector", lambda v, dc=dc, bk=bk, tau0=tau0:
                                v.tensor_copy(c_l2T[:, dc * TOK + tau0:dc * TOK + tau0 + 128],
                                              c_pt[:, bk * 1024:bk * 1024 + 128]),
                                waits=[t_tp], inc="sv")
                        t_l2T_wr[0] = t_tp_ev[(7) % 2]
                        t_rmm = None
                        for k in range(8):
                            t_rmm = P.step("tensor", lambda t, k=k, tau0=tau0:
                                t.matmul(c_psr[:],
                                         c_l2T[:, k * TOK + tau0:k * TOK + tau0 + 128],
                                         c_rT[:, k * 8:(k + 1) * 8],
                                         start=(k == 0), stop=(k == 7)),
                                waits=[t_tp_ev[0], t_tp_ev[1], t_psr_ev],
                                inc="sm" if k == 7 else None)
                        ti = s * 8 + lc
                        gap = lambda v: v.tensor_copy(c_sm[:, 40:48], c_st[:, 0:8])
                        P.step("vector", lambda v: v.reduce_max(c_st[:, 4:5], c_psr[:], AX.X),
                               waits=[t_rmm])
                        P.step("vector", gap)
                        t_rl = P.step("vector", lambda v: v.tensor_scalar(
                            c_sm[:, 0:8], c_psr[:], c_st[:, 4:5], None, ALU.subtract),
                            inc="sv")
                        t_psr_ev = t_rl
                        t_re = P.step("scalar", lambda sc: sc.activation(
                            out=c_sm[:, 0:8], in_=c_sm[:, 0:8], func=AF.Exp),
                            waits=[t_rl], inc="se")
                        P.step("vector", lambda v: v.reduce_sum(c_st[:, 5:6], c_sm[:, 0:8], AX.X),
                               waits=[t_re])
                        P.step("vector", gap)
                        P.step("vector", lambda v: v.reciprocal(c_st[:, 6:7], c_st[:, 5:6]))
                        P.step("vector", gap)
                        P.step("vector", lambda v: v.tensor_scalar(
                            c_sm[:, 0:8], c_sm[:, 0:8], c_st[:, 6:7], None, ALU.mult))
                        P.step("vector", gap)
                        P.step("vector", lambda v: v.reduce_max(c_st[:, 4:5], c_sm[:, 0:8], AX.X))
                        P.step("vector", gap)
                        P.step("vector", lambda v: v.tensor_scalar(
                            c_sm[:, 8:16], c_sm[:, 0:8], c_st[:, 4:5], None, ALU.is_equal))
                        P.step("vector", gap)
                        P.step("vector", lambda v: v.tensor_scalar(
                            c_sm[:, 8:16], c_sm[:, 8:16], -1e30, None, ALU.mult))
                        P.step("vector", gap)
                        P.step("vector", lambda v: v.tensor_add(
                            c_sm[:, 8:16], c_sm[:, 8:16], c_sm[:, 0:8]))
                        P.step("vector", gap)
                        P.step("vector", lambda v: v.reduce_max(c_st[:, 5:6], c_sm[:, 8:16], AX.X))
                        P.step("vector", gap)
                        P.step("vector", lambda v: v.tensor_scalar(
                            c_sm[:, 8:16], c_sm[:, 0:8], c_st[:, 5:6], None, ALU.is_ge))
                        P.step("vector", gap)
                        P.step("vector", lambda v, ti=ti: v.tensor_mul(
                            c_wexp[:, ti * 8:(ti + 1) * 8], c_sm[:, 0:8], c_sm[:, 8:16]))
            P.flush()

            if stop_after == "C1":
                _stop_here()
                return nc
            t_l2T_all = P.tok("sv")
            ESC2 = contextlib.ExitStack()
            with ESC2:
                c_w1 = ESC2.enter_context(nc.sbuf_tensor("c_w1", [128, 8 * DI], BF16))
                c_w2 = ESC2.enter_context(nc.sbuf_tensor("c_w2", [128, 16 * D], BF16))
                c_hm = ESC2.enter_context(nc.sbuf_tensor("c_hm", [128, 16 * 512], BF16))
                c_g1 = ESC2.enter_context(nc.sbuf_tensor("c_g1", [128, 512], F32))
                c_g2 = ESC2.enter_context(nc.sbuf_tensor("c_g2", [128, 512], F32))
                c_mo = ESC2.enter_context(nc.sbuf_tensor("c_mo", [128, 2 * 512], F32))
                c_pm = ESC2.enter_context(nc.psum_tensor("c_pm", [128, 2 * 512], F32))
                pbc2 = _pbw()
                P.step("vector", lambda v: v.nop(), waits=pbc2)
                P.step("scalar", lambda sc: sc.nop(), waits=pbc2)
                P.step("tensor", lambda t: t.nop(), waits=pbc2)
                t_w1_mm = None
                t_w2_mm = None
                t_pm_ev = [None, None]
                t_ar_prev = None
                t_mo_out = [None, None]
                t_mo_grid = [[None, None] for _ in range(32)]
                for e in range(8):
                    t_w1 = None
                    for k in range(8):
                        t_w1 = P.dma("gpsimd", c_w1[:, k * DI:(k + 1) * DI],
                                     moew1[e * D + k * 128:e * D + (k + 1) * 128, :],
                                     waits=[t_w1_mm] + (pbc2 if e == 0 else []))
                    t_w2 = None
                    for k in range(16):
                        t_w2 = P.dma("gpsimd", c_w2[:, k * D:(k + 1) * D],
                                     moew2[e * DI + k * 128:e * DI + (k + 1) * 128, :],
                                     waits=[t_w2_mm] + (pbc2 if e == 0 else []))
                    for tb in range(8):
                        for fc in range(16):
                            pb = fc % 2
                            t_mm = None
                            for k in range(8):
                                t_mm = P.step("tensor", lambda t, k=k, fc=fc, tb=tb, pb=pb:
                                    t.matmul(c_pm[:, pb * 512:(pb + 1) * 512],
                                             c_w1[:, k * DI + fc * 128:k * DI + fc * 128 + 128],
                                             c_l2T[:, k * TOK + tb * 512:k * TOK + tb * 512 + 512],
                                             start=(k == 0), stop=(k == 7)),
                                    waits=[t_w1, t_l2T_all, t_pm_ev[pb]],
                                    inc="sm" if k == 7 else None)
                            t_w1_mm = t_mm
                            t_sq = P.step("scalar", lambda sc, pb=pb: sc.activation(
                                out=c_g1[:], in_=c_pm[:, pb * 512:(pb + 1) * 512],
                                func=AF.Square), waits=[t_mm, t_ar_prev], inc="se")
                            P.step("vector", lambda v: v.tensor_scalar(
                                c_g2[:], c_g1[:], GELU_C, 1.0, ALU.mult, ALU.add),
                                waits=[t_sq])
                            t_ar = P.step("vector", lambda v, pb=pb: v.tensor_mul(
                                c_g2[:], c_g2[:], c_pm[:, pb * 512:(pb + 1) * 512]),
                                inc="sv")
                            t_ar_prev = t_ar
                            t_th = P.step("scalar", lambda sc: sc.activation(
                                out=c_g2[:], in_=c_g2[:], func=AF.Tanh, scale=GELU_S),
                                waits=[t_ar], inc="se")
                            P.step("vector", lambda v: v.tensor_scalar(
                                c_g2[:], c_g2[:], 0.5, 0.5, ALU.mult, ALU.add),
                                waits=[t_th])
                            t_hm = P.step("vector", lambda v, fc=fc, pb=pb: v.tensor_mul(
                                c_hm[:, fc * 512:(fc + 1) * 512], c_g2[:],
                                c_pm[:, pb * 512:(pb + 1) * 512]), inc="sv")
                            t_pm_ev[pb] = t_hm
                        t_hm_all = t_pm_ev[1]
                        t_hm_all0 = t_pm_ev[0]
                        for tt in range(4):
                            ti = tb * 4 + tt
                            for Fh in range(2):
                                pb = Fh
                                t_mm = None
                                for k2 in range(16):
                                    t_mm = P.step("tensor", lambda t, k2=k2, tt=tt, Fh=Fh, pb=pb:
                                        t.matmul(c_pm[:, pb * 512:(pb + 1) * 512],
                                                 c_hm[:, k2 * 512 + tt * 128:k2 * 512 + tt * 128 + 128],
                                                 c_w2[:, k2 * D + Fh * 512:k2 * D + Fh * 512 + 512],
                                                 start=(k2 == 0), stop=(k2 == 15)),
                                        waits=[t_w2, t_hm_all, t_hm_all0, t_pm_ev[pb]],
                                        inc="sm" if k2 == 15 else None)
                                t_w2_mm = t_mm
                                t_ev = P.step("vector", lambda v, ti=ti, e=e, Fh=Fh, pb=pb:
                                    v.tensor_scalar(c_mo[:, Fh * 512:(Fh + 1) * 512],
                                                    c_pm[:, pb * 512:(pb + 1) * 512],
                                                    c_wexp[:, ti * 8 + e:ti * 8 + e + 1],
                                                    None, ALU.mult),
                                    waits=[t_mm, t_mo_out[Fh]], inc="sv")
                                t_pm_ev[pb] = t_ev
                                tau0 = ti * 128
                                if e == 0:
                                    t_mo = P.dma("gpsimd",
                                                 MOE[tau0:tau0 + 128, Fh * 512:(Fh + 1) * 512],
                                                 c_mo[:, Fh * 512:(Fh + 1) * 512],
                                                 waits=[t_ev])
                                else:
                                    t_mo = P.dma("gpsimd",
                                                 MOE[tau0:tau0 + 128, Fh * 512:(Fh + 1) * 512],
                                                 c_mo[:, Fh * 512:(Fh + 1) * 512],
                                                 waits=[t_ev, t_mo_grid[ti][Fh]],
                                                 accum_op=ALU.add)
                                t_mo_grid[ti][Fh] = t_mo
                                t_mo_out[Fh] = t_mo
                t_moe_done = P.tok("sg")
                P.flush()

            if stop_after == "C2":
                _stop_here()
                return nc
            ESC3 = contextlib.ExitStack()
            with ESC3:
                c_wo = ESC3.enter_context(nc.sbuf_tensor("c_wo", [128, 16 * D], BF16))
                c_y3 = ESC3.enter_context(nc.sbuf_tensor("c_y3", [128, 16 * 128], BF16))
                c_z = ESC3.enter_context(nc.sbuf_tensor("c_z", [128, D], F32))
                c_xc3 = ESC3.enter_context(nc.sbuf_tensor("c_xc3", [128, D], F32))
                c_tt = ESC3.enter_context(nc.sbuf_tensor("c_tt", [128, D], F32))
                c_s = ESC3.enter_context(nc.sbuf_tensor("c_s", [128, D], F32))
                c_g = ESC3.enter_context(nc.sbuf_tensor("c_g", [128, D], F32))
                c_x3 = ESC3.enter_context(nc.sbuf_tensor("c_x3", [128, D], F32))
                c_st3 = ESC3.enter_context(nc.sbuf_tensor("c_st3", [128, 8], F32))
                c_pw = ESC3.enter_context(nc.psum_tensor("c_pw", [128, 2 * 512], F32))
                t_ha_all = P.tok("sl")
                pbc3 = _pbw()
                P.step("vector", lambda v: v.nop(), waits=pbc3)
                P.step("scalar", lambda sc: sc.nop(), waits=pbc3)
                P.step("tensor", lambda t: t.nop(), waits=pbc3)
                t_wo = None
                for k in range(16):
                    t_wo = P.dma("gpsimd", c_wo[:, k * D:(k + 1) * D],
                                 wo[k * 128:(k + 1) * 128, :], waits=pbc3)
                t_y3_mm = None
                t_pw_ev = None
                t_fin = None
                for ti in range(32):
                    tau0 = ti * 128
                    t_y3d = None
                    for k2 in range(16):
                        t_y3d = P.step("sync", lambda s, k2=k2, ti=ti:
                            s.dma_start(out=c_y3[:, k2 * 128:(k2 + 1) * 128],
                                        in_=Y3T_my[k2 * 128:(k2 + 1) * 128,
                                                   ti * 128:(ti + 1) * 128]),
                            waits=[t_cp_y3, t_y3_mm] + (pbc3 if ti == 0 else []),
                            inc="sl", amt=16)
                    t_mm = None
                    for k2 in range(16):
                        for Fh in range(2):
                            t_mm = P.step("tensor", lambda t, k2=k2, Fh=Fh:
                                t.matmul(c_pw[:, Fh * 512:(Fh + 1) * 512],
                                         c_y3[:, k2 * 128:(k2 + 1) * 128],
                                         c_wo[:, k2 * D + Fh * 512:k2 * D + Fh * 512 + 512],
                                         start=(k2 == 0), stop=(k2 == 15)),
                                waits=[t_y3d, t_wo, t_pw_ev],
                                inc="sm" if (k2 == 15 and Fh == 1) else None)
                    t_y3_mm = t_mm
                    t_xn3 = P.dma("sync", c_x3[:], XN[tau0:tau0 + 128, :],
                                  waits=[t_fin, t_ha_all])
                    P.step("vector", lambda v: v.tensor_add(c_z[:], c_pw[:, 0:1024],
                           rsl("bo")), waits=[t_mm, t_rep, t_xn3])
                    t_z = P.step("vector", lambda v: v.tensor_add(c_z[:], c_z[:], c_x3[:]),
                                 inc="sv")
                    t_pw_ev = t_z
                    t_var = emit_ln(P, None, c_z[:], c_xc3[:], c_tt[:], c_st3,
                                    ceps, [t_eps])
                    P.step("vector", lambda v: v.tensor_mul(c_s[:], c_xc3[:], rsl("gam")))
                    P.step("vector", lambda v: v.tensor_add(c_s[:], c_s[:], rsl("bet")))
                    t_ha3 = P.dma("sync", c_tt[:], HA[tau0:tau0 + 128, :],
                                  waits=[t_var, t_ha_all])
                    t_mo3 = P.dma("sync", c_g[:], MOE[tau0:tau0 + 128, :],
                                  waits=[t_moe_done])
                    t_ff = P.step("vector", lambda v: v.tensor_add(c_g[:], c_g[:], c_tt[:]),
                                  waits=[t_ha3, t_mo3], inc="sv")
                    t_si = P.step("scalar", lambda sc: sc.activation(
                        out=c_g[:], in_=c_g[:], func=AF.Silu), waits=[t_ff], inc="se")
                    t_x = P.dma("sync", c_x3[:], xloc[tau0:tau0 + 128, :], waits=[t_z])
                    P.step("vector", lambda v: v.tensor_mul(c_g[:], c_g[:], c_s[:]),
                           waits=[t_si, t_x])
                    t_o = P.step("vector", lambda v: v.tensor_add(c_xc3[:], c_g[:], c_x3[:]),
                                 inc="sv")
                    t_fin = P.dma("sync", out[tau0:tau0 + 128, :], c_xc3[:], waits=[t_o])
                P.step("sync", lambda s: _nopwait(s), waits=[P.tok("sl"), P.tok("sg")])
                P.flush()

    return nc


_NC_CACHE = {}


def kernel(L0_attn_x, L0_attn_Wi, L0_attn_bi, L0_attn_Wx, L0_attn_bx,
           L0_attn_Wdt, L0_attn_bdt, L0_attn_A_log, L0_attn_Dp, L0_attn_Wo,
           L0_attn_bo, L0_attn_gamma, L0_attn_beta, L0_ffn_ln1_weight,
           L0_ffn_ln1_bias, L0_ffn_ln2_weight, L0_ffn_ln2_bias,
           L0_ffn_qkv_weight, L0_ffn_proj_weight, L0_ffn_router_weight,
           L0_ffn_moe_w1, L0_ffn_moe_w2):
    x = np.ascontiguousarray(np.asarray(L0_attn_x, dtype=np.float32))
    f32 = lambda a: np.ascontiguousarray(np.asarray(a, dtype=np.float32))
    wi = f32(L0_attn_Wi)
    bi = f32(L0_attn_bi).reshape(F4, 1)
    wx = f32(L0_attn_Wx)
    bx = f32(L0_attn_bx).reshape(WXW, 1)
    wdt = f32(L0_attn_Wdt)
    bdt = f32(L0_attn_bdt).reshape(DI, 1)
    alog = f32(L0_attn_A_log)
    dp = f32(L0_attn_Dp).reshape(DI, 1)
    wo = f32(L0_attn_Wo)
    bo = f32(L0_attn_bo).reshape(1, D)
    gam = f32(L0_attn_gamma).reshape(1, D)
    bet = f32(L0_attn_beta).reshape(1, D)
    l1w = f32(L0_ffn_ln1_weight).reshape(1, D)
    l1b = f32(L0_ffn_ln1_bias).reshape(1, D)
    l2w = f32(L0_ffn_ln2_weight).reshape(1, D)
    l2b = f32(L0_ffn_ln2_bias).reshape(1, D)
    qkvw = f32(L0_ffn_qkv_weight)
    projw = f32(L0_ffn_proj_weight)
    routw = f32(L0_ffn_router_weight)
    moew1 = f32(L0_ffn_moe_w1).reshape(8 * D, DI)
    moew2 = f32(L0_ffn_moe_w2).reshape(8 * DI, D)

    if "nc" not in _NC_CACHE:
        _NC_CACHE["nc"] = build_program()
    nc = _NC_CACHE["nc"]

    in_maps = []
    for c in range(NC):
        sl = slice(LC * c, LC * (c + 1))
        dsl = slice(DSH * c, DSH * (c + 1))
        in_maps.append(dict(
            xloc=np.ascontiguousarray(x[:, sl, :].reshape(TOK, D)),
            wi=wi, bi=bi, wx=wx, bx=bx,
            wdts=np.ascontiguousarray(wdt[:, dsl]),
            bdts=np.ascontiguousarray(bdt[dsl]),
            alogs=np.ascontiguousarray(alog[dsl]),
            dps=np.ascontiguousarray(dp[dsl]),
            wo=wo, bo=bo, gam=gam, bet=bet,
            l1w=l1w, l1b=l1b, l2w=l2w, l2b=l2b,
            qkvw=qkvw, projw=projw, routw=routw,
            moew1=moew1, moew2=moew2,
            cid=np.array([[c]], dtype=np.int32),
        ))
    res = run_bass_kernel_spmd(nc, in_maps, core_ids=list(range(NC)))
    outp = np.empty((B, L, D), dtype=np.float32)
    for c in range(NC):
        outp[:, LC * c:LC * (c + 1), :] = res.results[c]["out"].reshape(B, LC, D)
    return outp
